# version 14 (final)
# baseline (speedup 1.0000x reference)
import math
from contextlib import ExitStack

import numpy as np
import ml_dtypes

import concourse.bass as bass
import concourse.mybir as mybir
from concourse.bass_utils import run_bass_kernel_spmd

F32 = mybir.dt.float32
BF16 = mybir.dt.bfloat16
F8 = mybir.dt.float8e4
AF = mybir.ActivationFunctionType
ALU = mybir.AluOpType
AX = mybir.AxisListType

D = 2048
L = 8192
TOK = 1024
NCH = 16
FF = 6144
R = 8
HALF = 512
SCALE = 1.0 / math.sqrt(128.0)
EPS = 1e-5
INV_D = 1.0 / D

ATT_GROUPS = [3] * 21 + [1]
ALL_PHASES = ("p1", "ln1", "qkv", "attn", "ffn")


class B:
    pass


def build(phases=ALL_PHASES, dbg=()):
    nc = bass.Bass("TRN2", target_bir_lowering=False)
    b = B()
    b.nc = nc
    b.phases = set(phases)
    b.dbg = set(dbg)

    din = lambda n, s, t: nc.dram_tensor(n, s, t, kind="ExternalInput").ap()
    b.xT = din("xT", [D, TOK], F32)
    b.wqT = din("wqT", [D, D], F32)
    b.wkT = din("wkT", [D, D], F32)
    b.wvT = din("wvT", [D, D], F32)
    b.bqc = din("bqc", [128, 16], F32)
    b.bkc = din("bkc", [128, 16], F32)
    b.bvr = din("bvr", [1, D], F32)
    b.gscc = din("gscc", [128, 48], F32)
    b.uscc = din("uscc", [128, 48], F32)
    b.dscc = din("dscc", [128, 16], F32)
    b.cvec = din("cvec", [1, D], F32)
    b.wmod_sl = din("wmod_sl", [D, 1536], F32)
    b.gate_sl = din("gate_sl", [D // R, FF], F32)
    b.up_sl = din("up_sl", [D // R, FF], F32)
    b.down_sl = din("down_sl", [FF // R, D], F32)
    b.id_f32 = din("id_f32", [128, 128], F32)
    b.ones_colb = din("ones_colb", [128, 1], BF16)
    b.ones_colf = din("ones_colf", [128, 1], F32)
    b.ones_rowf = din("ones_rowf", [1, 128], F32)

    b.out_d = nc.dram_tensor("out", [D, TOK], F32, kind="ExternalOutput").ap()

    b.dbg_d = {}
    for name, shape, dt in [
        ("modp", [128, 64], F32), ("gacol", [128, 16], F32), ("gbds", [128, 16], F32),
        ("h1", [128, NCH * TOK], BF16), ("q", [128, NCH * TOK], BF16),
        ("negc", [128, 16], F32), ("x1", [128, NCH * TOK], F32),
        ("h2", [128, NCH * TOK], BF16),
        ("kvb", [4096, TOK], BF16), ("kvg", [R * 4096, TOK], BF16),
    ]:
        if name in b.dbg:
            b.dbg_d[name] = nc.dram_tensor("dbg_" + name, shape, dt,
                                           kind="ExternalOutput").ap()

    b.mod_b = nc.dram_tensor("mod_b", [1, 1536], F32)
    b.mod_g = nc.dram_tensor("mod_g", [R, 1536], F32)
    b.kn_b = nc.dram_tensor("kn_b", [1, 16], F32)
    b.kn_g = nc.dram_tensor("kn_g", [1, 16], F32)
    b.kvb_K4 = [nc.dram_tensor(f"kvb_K4_{i}", [512, TOK], BF16) for i in range(4)]
    b.kvb_V4 = [nc.dram_tensor(f"kvb_V4_{i}", [1024, 512], BF16) for i in range(4)]
    b.kvg_K4 = [nc.dram_tensor(f"kvg_K4_{i}", [R * 512, TOK], BF16) for i in range(4)]
    b.kvg_V4 = [nc.dram_tensor(f"kvg_V4_{i}", [R * 1024, 512], BF16) for i in range(4)]
    b.g8b = nc.dram_tensor("g8b", [D // R, FF], F8)
    b.u8b = nc.dram_tensor("u8b", [D // R, FF], F8)
    b.d8b = nc.dram_tensor("d8b", [FF // R, D], F8)
    b.g8 = nc.dram_tensor("g8", [D, FF], F8)
    b.u8 = nc.dram_tensor("u8", [D, FF], F8)
    b.d8 = nc.dram_tensor("d8", [FF, D], F8)

    with ExitStack() as stack:
        e = stack.enter_context
        b.stack = stack
        b.x_fm = e(nc.sbuf_tensor("x_fm", [128, NCH * TOK], F32))
        b.id_sb = e(nc.sbuf_tensor("id_sb", [128, 128], F32))
        b.onecb = e(nc.sbuf_tensor("onecb", [128, 1], BF16))
        b.onecf = e(nc.sbuf_tensor("onecf", [128, 1], F32))
        b.onerf = e(nc.sbuf_tensor("onerf", [1, 128], F32))
        b.modp = e(nc.sbuf_tensor("modp", [128, 64], F32))
        b.gacol = e(nc.sbuf_tensor("gacol", [128, 16], F32))
        b.gbds = e(nc.sbuf_tensor("gbds", [128, 16], F32))
        b.bq_sb = e(nc.sbuf_tensor("bq_sb", [128, 16], F32))
        b.bk_sb = e(nc.sbuf_tensor("bk_sb", [128, 16], F32))
        b.gsc_sb = e(nc.sbuf_tensor("gsc_sb", [128, 48], F32))
        b.usc_sb = e(nc.sbuf_tensor("usc_sb", [128, 48], F32))
        b.dsc_sb = e(nc.sbuf_tensor("dsc_sb", [128, 16], F32))
        b.negc = e(nc.sbuf_tensor("negc", [128, 16], F32))
        b.qm2 = e(nc.sbuf_tensor("qm2", [1, 16], F32))
        b.nmA = e(nc.sbuf_tensor("nmA", [1, 16], F32))
        b.nmB = e(nc.sbuf_tensor("nmB", [1, 16], F32))
        b.rowa = e(nc.sbuf_tensor("rowa", [1, 512], F32))
        b.rowb = e(nc.sbuf_tensor("rowb", [1, 512], F32))
        b.rowc = e(nc.sbuf_tensor("rowc", [1, 512], F32))
        b.rowd = e(nc.sbuf_tensor("rowd", [1, 512], F32))
        b.rowe = e(nc.sbuf_tensor("rowe", [1, 16], F32))
        b.epsb = e(nc.sbuf_tensor("epsb", [128, 1], F32))
        b.cv_sb = e(nc.sbuf_tensor("cv_sb", [1, D], F32))

        class Sem:
            def __init__(self, name):
                self.h = e(nc.semaphore(name))
                self.v = 0
        b.Sem = Sem
        b.sPE = Sem("sPE")
        b.sACT = Sem("sACT")
        b.sDVE = Sem("sDVE")
        b.sCC = Sem("sCC")
        b.sST = Sem("sST")
        b.sBAR = Sem("sBAR")
        b.sGE = Sem("sGE")
        b.sDBG = Sem("sDBG")
        b.sCON = Sem("sCON")
        b.sXQ = [Sem(f"sXQ{i}") for i in range(4)]
        b.sWM = [Sem(f"sWM{i}") for i in range(2)]
        b.sF8 = Sem("sF8")
        b.sMS = Sem("sMS")
        b.sMG = Sem("sMG")
        b.sBV = Sem("sBV")
        b.sKS = Sem("sKS")
        b.sKN = Sem("sKN")
        b.sWP = [Sem(f"sWP{i}") for i in range(2)]
        b.sKT = [Sem(f"sKT{i}") for i in range(4)]
        b.sKV = [Sem(f"sKV{i}") for i in range(2)]
        b.sKL = [Sem(f"sKL{i}") for i in range(2)]
        b.sFP = [Sem(f"sFP{i}") for i in range(2)]
        b.sFPB = [Sem(f"sFPB{i}") for i in range(4)]
        b.all_dma_sems = ([b.sCON, b.sMS, b.sMG, b.sBV, b.sKS, b.sKN,
                           b.sST, b.sF8] + b.sXQ + b.sWM + b.sWP + b.sKT +
                          b.sKV + b.sKL + b.sFP + b.sFPB)

        b.block = e(nc.Block())
        _body(b)

        def fin(sy):
            sy.wait_ge(b.sST.h, b.sST.v)
        b.block.sync(fin)

        def fing(g):
            g.wait_ge(b.sDBG.h, b.sDBG.v)
        b.block.gpsimd(fing)
    return nc, b


def _bar(b, exclude=()):
    flush = [(s.h, s.v) for s in
             (b.all_dma_sems + [b.sPE, b.sACT, b.sDVE, b.sGE, b.sCC, b.sST, b.sDBG])
             if s.v > 0 and s not in exclude]
    for en in ("tensor", "scalar", "vector", "sync", "gpsimd"):
        def emit(eng):
            for hh, vv in flush:
                eng.wait_ge(hh, vv)
        getattr(b.block, en)(emit)


def _dump(b, name, src, sem, val):
    if name not in b.dbg_d:
        return
    dst = b.dbg_d[name]
    waits = sem if isinstance(sem, list) else [(sem, val)]

    def emit(g):
        for ss, vv in waits:
            g.wait_ge(ss.h, vv)
        g.dma_start(out=dst, in_=src).then_inc(b.sDBG.h, 16)
        b.sDBG.v += 16
    b.block.gpsimd(emit)


def _body(b):
    nc, block = b.nc, b.block

    def p0(sy):
        for dst, src in [(b.id_sb, b.id_f32), (b.onecb, b.ones_colb),
                         (b.onecf, b.ones_colf), (b.onerf, b.ones_rowf),
                         (b.bq_sb, b.bqc), (b.bk_sb, b.bkc),
                         (b.gsc_sb, b.gscc), (b.usc_sb, b.uscc),
                         (b.dsc_sb, b.dscc)]:
            sy.dma_start(out=dst[:, :], in_=src).then_inc(b.sCON.h, 16)
            b.sCON.v += 16
    block.sync(p0)

    def p0c(sy):
        sy.dma_start(out=b.cv_sb[:, :], in_=b.cvec).then_inc(b.sCON.h, 16)
        b.sCON.v += 16
    if "p1" in b.phases:
        block.sync(p0c)
    b.m_consts = b.sCON.v

    def p0e(v):
        v.memset(b.epsb[:, :], EPS).then_inc(b.sDVE.h, 1)
        b.sDVE.v += 1
    block.vector(p0e)
    b.m_eps = b.sDVE.v

    b.m_mod_ready = None
    if "p1" in b.phases:
        with ExitStack() as p1_scope:
            b.p1_scope = p1_scope
            _mod(b)
        _bar(b, exclude=tuple([b.sF8, b.sCC] + b.sXQ))

    def p0x(sy):
        for ic in range(NCH):
            sy.dma_start(out=b.x_fm[:, ic * TOK:(ic + 1) * TOK],
                         in_=b.xT[ic * 128:(ic + 1) * 128, :]).then_inc(
                             b.sXQ[ic // 4].h, 16)
            b.sXQ[ic // 4].v += 16
    block.sync(p0x)
    b.m_x_quarters = [s4.v for s4 in b.sXQ]

    if "ln1" in b.phases or "qkv" in b.phases:
        with nc.sbuf_tensor("q_all", [128, NCH * TOK], BF16) as q_all:
            b.q_all = q_all
            _qkv(b)
            _bar(b, exclude=(b.sCC,))
            if "attn" in b.phases:
                _attn(b)
                _bar(b)
        _dump(b, "x1", b.x_fm[:, :], b.sDVE, b.sDVE.v)

    if "ffn" in b.phases:
        _ffn(b)
    else:
        _store_x(b)


def _store_x(b):
    def emit(sy):
        for s4 in b.sXQ:
            sy.wait_ge(s4.h, s4.v)
        sy.wait_ge(b.sDVE.h, b.sDVE.v)
        for ic in range(NCH):
            sy.dma_start(out=b.out_d[ic * 128:(ic + 1) * 128, :],
                         in_=b.x_fm[:, ic * TOK:(ic + 1) * TOK]).then_inc(b.sST.h, 16)
            b.sST.v += 16
    b.block.sync(emit)


def _mod(b):
    nc, block = b.nc, b.block
    sPE, sACT, sDVE, sCC = b.sPE, b.sACT, b.sDVE, b.sCC
    cv_sb = b.cv_sb

    if True:
        e = b.p1_scope.enter_context
        sc_sb = e(nc.sbuf_tensor("sc_sb", [1, D], F32))
        scT = e(nc.sbuf_tensor("scT", [128, 16], F32))
        wm_sb = e(nc.sbuf_tensor("wm_sb", [128, 2 * 1536], F32))
        mod_sb = e(nc.sbuf_tensor("mod_sb", [1, 1536], F32))
        scT_ps = e(nc.psum_tensor("scT_ps", [128, 16], F32))
        mod_ps = e(nc.psum_tensor("mod_ps", [1, 1536], F32))

        def silu(s):
            s.wait_ge(b.sCON.h, b.m_consts)
            s.activation(out=sc_sb[:, :], in_=cv_sb[:, :], func=AF.Silu).then_inc(sACT.h, 1)
            b.sACT.v += 1
        block.scalar(silu)
        m_silu = b.sACT.v

        def tr16(t):
            t.wait_ge(sACT.h, m_silu)
            t.wait_ge(b.sCON.h, b.m_consts)
            for k in range(16):
                ins = t.transpose(out=scT_ps[:, k:k + 1],
                                  in_=sc_sb[:, k * 128:(k + 1) * 128],
                                  identity=b.id_sb[0:1, 0:1])
            ins.then_inc(sPE.h, 1)
            b.sPE.v += 1
        block.tensor(tr16)
        m_tr = b.sPE.v

        def cp(v):
            v.wait_ge(sPE.h, m_tr)
            v.tensor_copy(out=scT[:, :], in_=scT_ps[:, :]).then_inc(sDVE.h, 1)
            b.sDVE.v += 1
        block.vector(cp)
        m_scT = b.sDVE.v

        wm_marks = []
        for ic in range(NCH):
            par = ic % 2

            swm = b.sWM[par]

            def ldw(sy, ic=ic, par=par, swm=swm):
                if ic >= 2:
                    sy.wait_ge(sPE.h, wm_marks[ic - 2])
                sy.dma_start(out=wm_sb[:, par * 1536:(par + 1) * 1536],
                             in_=b.wmod_sl[ic * 128:(ic + 1) * 128, :]).then_inc(swm.h, 16)
                swm.v += 16
            block.sync(ldw)
            m_ldw = swm.v

            def mm(t, ic=ic, par=par, m_ldw=m_ldw, swm=swm):
                t.wait_ge(swm.h, m_ldw)
                if ic == 0:
                    t.wait_ge(sDVE.h, m_scT)
                for s3 in range(3):
                    ins = t.matmul(out=mod_ps[:, s3 * 512:(s3 + 1) * 512],
                                   lhsT=scT[:, ic:ic + 1],
                                   rhs=wm_sb[:, par * 1536 + s3 * 512:par * 1536 + (s3 + 1) * 512],
                                   start=(ic == 0), stop=(ic == NCH - 1))
                ins.then_inc(sPE.h, 1)
                b.sPE.v += 1
            block.tensor(mm)
            wm_marks.append(b.sPE.v)
        m_mm = b.sPE.v

        def evac(v):
            v.wait_ge(sPE.h, m_mm)
            v.tensor_copy(out=mod_sb[:, :], in_=mod_ps[:, :]).then_inc(sDVE.h, 1)
            b.sDVE.v += 1
        block.vector(evac)
        m_msb = b.sDVE.v

        def cc1(g):
            g.wait_ge(sDVE.h, m_msb)
            g.dma_start(out=b.mod_b.ap(), in_=mod_sb[:, :]).then_inc(b.sMS.h, 16)
            b.sMS.v += 16
            g.wait_ge(b.sMS.h, b.sMS.v)
            g.collective_compute("AllGather", ALU.bypass,
                                 replica_groups=[list(range(R))],
                                 ins=[b.mod_b.ap().opt()],
                                 outs=[b.mod_g.ap().opt()]).then_inc(sCC.h, 1)
            b.sCC.v += 1
        block.gpsimd(cc1)
        b.m_cc1 = b.sCC.v



def _ln(b, sc, dst_all, s_off, c_off, x_wait=None, x_chunks=None):
    nc, block = b.nc, b.block
    sPE, sACT, sDVE = b.sPE, b.sACT, b.sDVE
    A, Bp, C, Dp = sc["A"], sc["B"], sc["C"], sc["D"]
    xbf, xsq, u_sb = sc["xbf"], sc["xsq"], sc["u"]
    h_marks = [[None] * NCH, [None] * NCH]
    rows_done = {}
    apply_done = {}
    for half in range(2):
        peb = b.sPE.v
        actb = b.sACT.v
        pe_marks = [peb + ic + 1 for ic in range(NCH)]
        act_marks = [actb + ic + 1 for ic in range(NCH)]

        def act_sec(s, half=half, pe_marks=pe_marks):
            for ic in range(NCH):
                if x_chunks is not None and half == 0:
                    s.wait_ge(b.sXQ[ic // 4].h, x_chunks[ic // 4])
                elif x_wait is not None and ic == 0 and half == 0:
                    s.wait_ge(x_wait[0].h, x_wait[1])
                if ic >= 2:
                    s.wait_ge(sPE.h, pe_marks[ic - 2])
                src = b.x_fm[:, ic * TOK + half * HALF: ic * TOK + half * HALF + HALF]
                s.activation(out=xbf[:, (ic % 2) * HALF:(ic % 2 + 1) * HALF],
                             in_=src, func=AF.Identity)
                s.activation(out=xsq[:, (ic % 2) * HALF:(ic % 2 + 1) * HALF],
                             in_=src, func=AF.Square).then_inc(sACT.h, 1)
                b.sACT.v += 1
        block.scalar(act_sec)

        def pe_sec(t, half=half, act_marks=act_marks):
            for ic in range(NCH):
                t.wait_ge(sACT.h, act_marks[ic])
                if ic == 0 and half == 0:
                    t.wait_ge(b.sCON.h, b.m_consts)
                if ic == 0 and half == 1:
                    t.wait_ge(sDVE.h, rows_done[0])
                t.matmul(out=A[:, :], lhsT=b.onecb[:, :],
                         rhs=xbf[:, (ic % 2) * HALF:(ic % 2 + 1) * HALF],
                         start=(ic == 0), stop=(ic == NCH - 1))
                t.matmul(out=Bp[:, :], lhsT=b.onecb[:, :],
                         rhs=xsq[:, (ic % 2) * HALF:(ic % 2 + 1) * HALF],
                         start=(ic == 0), stop=(ic == NCH - 1)).then_inc(sPE.h, 1)
                b.sPE.v += 1
        block.tensor(pe_sec)
        assert b.sPE.v == pe_marks[-1]
        m_stats = b.sPE.v

        def rows(v):
            v.wait_ge(sPE.h, m_stats)
            v.tensor_scalar(out=b.rowa[:, :], in0=A[:, :], scalar1=INV_D,
                            scalar2=None, op0=ALU.mult)
            v.tensor_scalar(out=b.rowb[:, :], in0=Bp[:, :], scalar1=INV_D,
                            scalar2=None, op0=ALU.mult).then_inc(sDVE.h, 1)
            b.sDVE.v += 1
            v.wait_ge(sDVE.h, b.sDVE.v)
            v.tensor_tensor(out=b.rowc[:, :], in0=b.rowa[:, :], in1=b.rowa[:, :],
                            op=ALU.mult).then_inc(sDVE.h, 1)
            b.sDVE.v += 1
            v.wait_ge(sDVE.h, b.sDVE.v)
            v.tensor_tensor(out=b.rowc[:, :], in0=b.rowb[:, :], in1=b.rowc[:, :],
                            op=ALU.subtract).then_inc(sDVE.h, 1)
            b.sDVE.v += 1
        block.vector(rows)
        m_var = b.sDVE.v

        def rstd(s):
            s.wait_ge(sDVE.h, m_var)
            s.activation(out=b.rowd[:, :], in_=b.rowc[:, :], func=AF.Ln,
                         bias=b.epsb[0:1, :], scale=1.0).then_inc(sACT.h, 1)
            b.sACT.v += 1
            s.wait_ge(sACT.h, b.sACT.v)
            s.activation(out=b.rowd[:, :], in_=b.rowd[:, :], func=AF.Exp,
                         scale=-0.5).then_inc(sACT.h, 1)
            b.sACT.v += 1
        block.scalar(rstd)
        m_rstd = b.sACT.v

        def r2(v):
            v.wait_ge(sACT.h, m_rstd)
            v.scalar_tensor_tensor(out=b.rowc[:, :], in0=b.rowa[:, :], scalar=-1.0,
                                   in1=b.rowd[:, :], op0=ALU.mult,
                                   op1=ALU.mult).then_inc(sDVE.h, 1)
            b.sDVE.v += 1
        block.vector(r2)
        rows_done[half] = b.sDVE.v
        m_r2 = b.sDVE.v

        def bc(t, half=half):
            t.wait_ge(sDVE.h, m_r2)
            if half == 1:
                t.wait_ge(sDVE.h, apply_done[0])
            t.matmul(out=C[:, :], lhsT=b.onerf[:, :], rhs=b.rowd[:, :],
                     start=True, stop=True)
            t.matmul(out=Dp[:, :], lhsT=b.onerf[:, :], rhs=b.rowc[:, :],
                     start=True, stop=True).then_inc(sPE.h, 1)
            b.sPE.v += 1
        block.tensor(bc)
        m_bc = b.sPE.v

        def apply(v, half=half):
            v.wait_ge(sPE.h, m_bc)
            for ic in range(NCH):
                src = b.x_fm[:, ic * TOK + half * HALF: ic * TOK + half * HALF + HALF]
                dst = dst_all[:, ic * TOK + half * HALF: ic * TOK + half * HALF + HALF]
                up = u_sb[:, (ic % 2) * HALF:(ic % 2 + 1) * HALF]
                v.tensor_tensor(out=up, in0=src, in1=C[:, :], op=ALU.mult).then_inc(sDVE.h, 1)
                b.sDVE.v += 1
                v.wait_ge(sDVE.h, b.sDVE.v)
                v.tensor_tensor(out=up, in0=up, in1=Dp[:, :], op=ALU.add).then_inc(sDVE.h, 1)
                b.sDVE.v += 1
                v.wait_ge(sDVE.h, b.sDVE.v)
                v.tensor_scalar(out=dst, in0=up,
                                scalar1=b.modp[:, c_off + ic:c_off + ic + 1],
                                scalar2=b.modp[:, s_off + ic:s_off + ic + 1],
                                op0=ALU.mult, op1=ALU.add).then_inc(sDVE.h, 1)
                b.sDVE.v += 1
                h_marks[half][ic] = b.sDVE.v
        block.vector(apply)
        apply_done[half] = b.sDVE.v
    sc["apply_done"] = apply_done
    return h_marks


def _qkv(b):
    nc, block = b.nc, b.block
    sPE, sACT, sDVE, sCC = b.sPE, b.sACT, b.sDVE, b.sCC
    q_all = b.q_all

    with ExitStack() as scm:
        e = scm.enter_context
        h_all = e(nc.sbuf_tensor("h_all", [128, NCH * TOK], BF16))
        xbf = e(nc.sbuf_tensor("ln_xbf", [128, 2 * HALF], BF16))
        xsq = e(nc.sbuf_tensor("ln_xsq", [128, 2 * HALF], BF16))
        u_sb = e(nc.sbuf_tensor("ln_u", [128, 2 * HALF], F32))
        wp = e(nc.sbuf_tensor("wp", [128, 2 * 8192], BF16))
        kst = e(nc.sbuf_tensor("kst", [128, 4 * HALF], BF16))
        nsq = e(nc.sbuf_tensor("nsq", [128, 2 * HALF], BF16))
        bvb = e(nc.sbuf_tensor("bvb", [128, D], F32))
        bvrow = b.cv_sb
        A = e(nc.psum_tensor("ps_A", [1, 512], F32))
        Bp = e(nc.psum_tensor("ps_B", [1, 512], F32))
        C = e(nc.psum_tensor("ps_C", [128, 512], F32))
        Dp = e(nc.psum_tensor("ps_D", [128, 512], F32))
        Ep = e(nc.psum_tensor("ps_E", [128, 512], F32))
        Fp = e(nc.psum_tensor("ps_F", [128, 512], F32))
        Gp = e(nc.psum_tensor("ps_G", [128, 512], F32))
        Hp = e(nc.psum_tensor("ps_H", [128, 512], F32))

        def ld_w_panel(wT, pan, gate=None):
            par = pan % 2
            swp = b.sWP[par]

            def ld(g, wT=wT, pan=pan, par=par, swp=swp, gate=gate):
                if gate is not None:
                    g.wait_ge(gate[0].h, gate[1])
                for g4 in range(4):
                    src = wT.rearrange("(g ic p) o -> g p ic o", ic=4, p=128)[
                        g4, :, :, pan * 512:(pan + 1) * 512]
                    dst = wp[:, par * 8192 + g4 * 2048: par * 8192 + (g4 + 1) * 2048] \
                        .rearrange("p (ic o) -> p ic o", o=512)
                    g.dma_start(out=dst, in_=src).then_inc(swp.h, 16)
                    swp.v += 16
            block.gpsimd(ld)
            return swp.v

        if "qkv" in b.phases:
            k_pre = {0: ld_w_panel(b.wkT, 0), 1: ld_w_panel(b.wkT, 1)}

        mg_sb = e(nc.sbuf_tensor("mg_sb", [96, 128], F32))
        mT_sb = e(nc.sbuf_tensor("mT_sb", [128, 96], F32))
        sgn_sb = e(nc.sbuf_tensor("sgn_sb", [128, 32], F32))
        mag_sb = e(nc.sbuf_tensor("mag_sb", [128, 32], F32))

        def ldmg(sy):
            sy.wait_ge(sCC.h, b.m_cc1)
            sy.dma_start(out=mg_sb[:, :],
                         in_=b.mod_g.ap().rearrange("r (a c) -> (r a) c", c=128)).then_inc(b.sMG.h, 16)
            b.sMG.v += 16
        block.sync(ldmg)
        m_mg = b.sMG.v

        def trm(t):
            t.wait_ge(b.sMG.h, m_mg)
            t.transpose(out=C[:, 0:96], in_=mg_sb[:, :],
                        identity=b.id_sb[0:96, 0:96]).then_inc(sPE.h, 1)
            b.sPE.v += 1
        block.tensor(trm)
        m_trm = b.sPE.v

        def cpm(v):
            v.wait_ge(sPE.h, m_trm)
            v.tensor_copy(out=mT_sb[:, :], in_=C[:, 0:96]).then_inc(sDVE.h, 1)
            b.sDVE.v += 1
        block.vector(cpm)
        m_mT = b.sDVE.v

        def ge(g):
            g.wait_ge(sDVE.h, m_mT)
            g.tensor_scalar(out=sgn_sb[:, 0:16], in0=mT_sb[:, 32:48],
                            scalar1=0.0, scalar2=None, op0=ALU.is_ge)
            g.tensor_scalar(out=sgn_sb[:, 16:32], in0=mT_sb[:, 80:96],
                            scalar1=0.0, scalar2=None, op0=ALU.is_ge).then_inc(b.sGE.h, 1)
            b.sGE.v += 1
        block.gpsimd(ge)
        m_ge = b.sGE.v

        def ab(s):
            s.wait_ge(sDVE.h, m_mT)
            s.activation(out=mag_sb[:, 0:16], in_=mT_sb[:, 32:48], func=AF.Abs)
            s.activation(out=mag_sb[:, 16:32], in_=mT_sb[:, 80:96], func=AF.Abs).then_inc(sACT.h, 1)
            b.sACT.v += 1
        block.scalar(ab)
        m_abs = b.sACT.v

        def fold(v):
            v.wait_ge(b.sGE.h, m_ge)
            v.wait_ge(sACT.h, m_abs)
            v.wait_ge(b.sCON.h, b.m_consts)
            v.tensor_scalar(out=sgn_sb[:, 0:32], in0=sgn_sb[:, 0:32],
                            scalar1=2.0, scalar2=-1.0, op0=ALU.mult, op1=ALU.add)
            v.tensor_scalar(out=mag_sb[:, 0:32], in0=mag_sb[:, 0:32],
                            scalar1=0.25, scalar2=None, op0=ALU.max).then_inc(sDVE.h, 1)
            b.sDVE.v += 1
            v.wait_ge(sDVE.h, b.sDVE.v)
            v.tensor_tensor(out=b.gacol[:, :], in0=sgn_sb[:, 0:16],
                            in1=mag_sb[:, 0:16], op=ALU.mult)
            v.tensor_tensor(out=b.gbds[:, :], in0=sgn_sb[:, 16:32],
                            in1=mag_sb[:, 16:32], op=ALU.mult).then_inc(sDVE.h, 1)
            b.sDVE.v += 1
            v.wait_ge(sDVE.h, b.sDVE.v)
            v.tensor_tensor(out=b.gbds[:, :], in0=b.gbds[:, :],
                            in1=b.dsc_sb[:, :], op=ALU.mult)
            v.tensor_copy(out=b.modp[:, 0:16], in_=mT_sb[:, 0:16])
            v.tensor_scalar(out=b.modp[:, 16:32], in0=mT_sb[:, 16:32],
                            scalar1=1.0, scalar2=None, op0=ALU.add)
            v.tensor_copy(out=b.modp[:, 32:48], in_=mT_sb[:, 48:64])
            v.tensor_scalar(out=b.modp[:, 48:64], in0=mT_sb[:, 64:80],
                            scalar1=1.0, scalar2=None, op0=ALU.add).then_inc(sDVE.h, 1)
            b.sDVE.v += 1
        block.vector(fold)
        b.m_mod_ready = b.sDVE.v

        _dump(b, "modp", b.modp[:, :], sDVE, b.m_mod_ready)
        _dump(b, "gacol", b.gacol[:, :], sDVE, b.m_mod_ready)
        _dump(b, "gbds", b.gbds[:, :], sDVE, b.m_mod_ready)

        sc = {"A": A, "B": Bp, "C": Gp, "D": Hp, "xbf": xbf, "xsq": xsq, "u": u_sb}
        h1m = _ln(b, sc, h_all, s_off=0, c_off=16, x_chunks=b.m_x_quarters)
        b.m_h1 = b.sDVE.v
        _dump(b, "h1", h_all[:, :], sDVE, b.m_h1)

        if "qkv" not in b.phases:
            def drain(v):
                v.wait_ge(sDVE.h, b.sDVE.v)
            block.vector(drain)
            return

        m_ln1_apply = sc["apply_done"][1]

        def ldbv(sy):
            sy.dma_start(out=bvrow[:, :], in_=b.bvr).then_inc(b.sBV.h, 16)
            b.sBV.v += 16
        block.sync(ldbv)
        m_bvr = b.sBV.v

        bvb_marks = []
        for s4 in range(4):
            def bvmm(t, s4=s4):
                t.wait_ge(b.sBV.h, m_bvr)
                t.wait_ge(b.sCON.h, b.m_consts)
                if s4 > 0:
                    t.wait_ge(sDVE.h, bvb_marks[s4 - 1])
                t.matmul(out=Ep[:, :], lhsT=b.onerf[:, :],
                         rhs=bvrow[:, s4 * 512:(s4 + 1) * 512],
                         start=True, stop=True).then_inc(sPE.h, 1)
                b.sPE.v += 1
            block.tensor(bvmm)
            m_mmv = b.sPE.v

            def bvcp(v, s4=s4, m_mmv=m_mmv):
                v.wait_ge(sPE.h, m_mmv)
                v.tensor_copy(out=bvb[:, s4 * 512:(s4 + 1) * 512],
                              in_=Ep[:, :]).then_inc(sDVE.h, 1)
                b.sDVE.v += 1
            block.vector(bvcp)
            bvb_marks.append(b.sDVE.v)
        m_bvb = b.sDVE.v

        def proj(which, wT, bias_sb, norm_dst, pan_gates, pan_hooks=None,
                 st_box=None, pre_pans=None, h_chunk_marks=None):
            evac_marks = []
            sq_marks = []
            nmm_marks = []
            red_marks = []
            st_marks = []
            pan_marks = {}
            proj_dve0 = b.sDVE.v
            if which == "k":
                kt0 = [s.v for s in b.sKT]
                for idx in range(32):
                    st_marks.append(kt0[idx % 4] + 16 * (idx // 4 + 1))
                if st_box is not None:
                    st_box.append(st_marks)
            for pan in range(4):
                par = pan % 2
                swp = b.sWP[par]
                if pan_hooks and pan in pan_hooks:
                    pan_hooks[pan]()
                if pre_pans and pan in pre_pans:
                    m_pan = pre_pans[pan]
                elif pan >= 2:
                    m_pan = ld_w_panel(wT, pan, (sPE, pan_marks[pan - 2]))
                else:
                    m_pan = ld_w_panel(wT, pan,
                                       pan_gates[pan] if pan_gates else None)

                pan_pe0 = b.sPE.v
                pan_act0 = b.sACT.v
                pe_order = [("m", 0), ("m", 1)]
                for i in range(8):
                    if i + 2 < 8:
                        pe_order.append(("m", i + 2))
                    pe_order.append(("n", i))
                mmz_marks = [None] * 8
                nmm_loc = [None] * 8
                for pos, (kind, i) in enumerate(pe_order):
                    if kind == "m":
                        mmz_marks[i] = pan_pe0 + pos + 1
                    else:
                        nmm_loc[i] = pan_pe0 + pos + 1
                for i in range(8):
                    evac_marks.append(pan_act0 + 2 * i + 1)
                    sq_marks.append(pan_act0 + 2 * i + 2)
                    red_marks.append(proj_dve0 + pan * 8 + i + 1)
                    nmm_marks.append(nmm_loc[i])

                def pe_sec(t, pan=pan, par=par, m_pan=m_pan, swp=swp,
                           pe_order=pe_order):
                    for kind, i in pe_order:
                        idx = pan * 8 + i
                        if kind == "m":
                            dcq, half = i // 2, i % 2
                            ps = C if idx % 2 == 0 else Dp
                            t.wait_ge(swp.h, m_pan)
                            if idx == 0 and h_chunk_marks is None:
                                t.wait_ge(sDVE.h, m_ln1_apply)
                            if idx >= 2:
                                t.wait_ge(sACT.h, evac_marks[idx - 2])
                            for ic in range(NCH):
                                if idx <= 1 and h_chunk_marks is not None:
                                    t.wait_ge(sDVE.h, h_chunk_marks[half][ic])
                                ins = t.matmul(out=ps[:, :],
                                               lhsT=wp[:, par * 8192 + ic * 512 + dcq * 128:
                                                       par * 8192 + ic * 512 + (dcq + 1) * 128],
                                               rhs=h_all[:, ic * TOK + half * HALF:
                                                         ic * TOK + half * HALF + HALF],
                                               start=(ic == 0), stop=(ic == NCH - 1))
                            ins.then_inc(sPE.h, 1)
                            b.sPE.v += 1
                        else:
                            t.wait_ge(sACT.h, sq_marks[idx])
                            if idx >= 1:
                                t.wait_ge(sDVE.h, red_marks[idx - 1])
                            t.matmul(out=A[:, :], lhsT=b.onecb[:, :],
                                     rhs=nsq[:, (idx % 2) * HALF:(idx % 2 + 1) * HALF],
                                     start=True, stop=True).then_inc(sPE.h, 1)
                            b.sPE.v += 1
                block.tensor(pe_sec)
                assert b.sPE.v == pan_pe0 + 16
                pan_marks[pan] = b.sPE.v

                def act_sec(s, pan=pan, par=par, which=which, bias_sb=bias_sb,
                            mmz_marks=mmz_marks):
                    for i in range(8):
                        idx = pan * 8 + i
                        dcq, half = i // 2, i % 2
                        dc = pan * 4 + dcq
                        ps = C if idx % 2 == 0 else Dp
                        s.wait_ge(sPE.h, mmz_marks[i])
                        if which == "k":
                            if idx >= 4:
                                s.wait_ge(b.sKT[idx % 4].h, st_marks[idx - 4])
                            dst = kst[:, (idx % 4) * HALF:(idx % 4 + 1) * HALF]
                        else:
                            dst = q_all[:, dc * TOK + half * HALF:
                                        dc * TOK + half * HALF + HALF]
                        s.activation(out=dst, in_=ps[:, :], func=AF.Identity,
                                     bias=bias_sb[:, dc:dc + 1]).then_inc(sACT.h, 1)
                        b.sACT.v += 1
                        s.wait_ge(sACT.h, b.sACT.v)
                        if idx >= 2:
                            s.wait_ge(sPE.h, nmm_marks[idx - 2])
                        s.activation(out=nsq[:, (idx % 2) * HALF:(idx % 2 + 1) * HALF],
                                     in_=dst, func=AF.Square).then_inc(sACT.h, 1)
                        b.sACT.v += 1
                block.scalar(act_sec)
                assert b.sACT.v == sq_marks[-1]

                if which == "k":
                    def gp_st(sy, pan=pan):
                        for i in range(8):
                            idx = pan * 8 + i
                            dcq, half = i // 2, i % 2
                            dc = pan * 4 + dcq
                            skt = b.sKT[idx % 4]
                            sy.wait_ge(sACT.h, evac_marks[idx])
                            sy.dma_start(
                                out=b.kvb_K4[dc // 4].ap()[(dc % 4) * 128:(dc % 4 + 1) * 128,
                                                           half * HALF:(half + 1) * HALF],
                                in_=kst[:, (idx % 4) * HALF:(idx % 4 + 1) * HALF]
                            ).then_inc(skt.h, 16)
                            skt.v += 16
                            assert skt.v == st_marks[idx]
                    block.sync(gp_st)

                def dve_red(v, pan=pan, norm_dst=norm_dst):
                    for i in range(8):
                        idx = pan * 8 + i
                        dcq, half = i // 2, i % 2
                        dc = pan * 4 + dcq
                        v.wait_ge(sPE.h, nmm_marks[idx])
                        v.reduce_max(out=norm_dst[half][:, dc:dc + 1], in_=A[:, :],
                                     axis=AX.X).then_inc(sDVE.h, 1)
                        b.sDVE.v += 1
                block.vector(dve_red)
                assert b.sDVE.v == red_marks[-1]
            return evac_marks, red_marks, st_marks, pan_marks

        def cc(ins_t, outs_t, gate):
            def emit(g):
                for hh, vv in gate:
                    g.wait_ge(hh, vv)
                g.collective_compute("AllGather", ALU.bypass,
                                     replica_groups=[list(range(R))],
                                     ins=[ins_t.ap().opt()],
                                     outs=[outs_t.ap().opt()]).then_inc(sCC.h, 1)
                b.sCC.v += 1
            block.gpsimd(emit)
            return b.sCC.v

        kst_marks_box = []

        def emit_ccK0():
            gate = [(b.sKT[i].h, kst_marks_box[0][4 + i]) for i in range(4)]
            b.m_ccK4[0] = cc(b.kvb_K4[0], b.kvg_K4[0], gate)

        b.m_ccK4 = [None] * 4
        b.m_ccV4 = [None] * 4
        k_evac, k_red, k_st, k_pans = proj("k", b.wkT, b.bk_sb, (b.nmA, b.nmB),
                                           None, pan_hooks={2: emit_ccK0},
                                           st_box=kst_marks_box, pre_pans=k_pre,
                                           h_chunk_marks=h1m)
        m_k_done = b.sPE.v
        kt_after_k = [s.v for s in b.sKT]

        def knfold(v):
            v.wait_ge(sDVE.h, k_red[-1])
            v.wait_ge(sDVE.h, b.sDVE.v)
            v.tensor_tensor(out=b.rowe[:, :], in0=b.nmA[:, 0:16],
                            in1=b.nmB[:, 0:16], op=ALU.max).then_inc(sDVE.h, 1)
            b.sDVE.v += 1
        block.vector(knfold)
        m_knrow = b.sDVE.v

        def cc2_emit(g):
            g.wait_ge(sDVE.h, m_knrow)
            g.dma_start(out=b.kn_b.ap(), in_=b.rowe[:, :]).then_inc(b.sKS.h, 16)
            b.sKS.v += 16
            g.wait_ge(b.sKS.h, b.sKS.v)
            g.collective_compute("AllReduce", ALU.max,
                                 replica_groups=[list(range(R))],
                                 ins=[b.kn_b.ap().opt()],
                                 outs=[b.kn_g.ap().opt()]).then_inc(sCC.h, 1)
            b.sCC.v += 1

        def ldv_emit(blk, gate_val):
            return ld_w_panel(b.wvT, blk, (sPE, gate_val))

        m_vpan_pre = ldv_emit(0, k_pans[2])

        block.gpsimd(cc2_emit)
        b.m_cc2 = b.sCC.v

        v_marks, vst_marks = [], []
        vpan_marks = {}
        for blk in range(4):
            par = blk % 2
            swp = b.sWP[par]
            if blk == 2:
                gate = [(b.sKT[i].h, vst_marks[4 + i]) for i in range(4)]
                b.m_ccV4[0] = cc(b.kvb_V4[0], b.kvg_V4[0], gate)

            if blk == 0:
                m_vpan = m_vpan_pre
            else:
                m_vpan = ldv_emit(blk, m_k_done if blk == 1 else vpan_marks[blk - 2])

            for tt in range(8):
                idx = blk * 8 + tt
                ps = Ep if idx % 2 == 0 else Fp

                def vmm(t, par=par, tt=tt, idx=idx, ps=ps, m_vpan=m_vpan, swp=swp):
                    t.wait_ge(swp.h, m_vpan)
                    if idx <= 1:
                        t.wait_ge(sDVE.h, m_bvb)
                    if idx >= 2:
                        t.wait_ge(sDVE.h, v_marks[idx - 2])
                    for ic in range(NCH):
                        ins = t.matmul(out=ps[:, :],
                                       lhsT=h_all[:, ic * TOK + tt * 128: ic * TOK + (tt + 1) * 128],
                                       rhs=wp[:, par * 8192 + ic * 512: par * 8192 + (ic + 1) * 512],
                                       start=(ic == 0), stop=(ic == NCH - 1))
                    ins.then_inc(sPE.h, 1)
                    b.sPE.v += 1
                block.tensor(vmm)
                m_vmm = b.sPE.v
                if tt == 7:
                    vpan_marks[blk] = m_vmm

                def vev(v, blk=blk, idx=idx, ps=ps, m_vmm=m_vmm):
                    v.wait_ge(sPE.h, m_vmm)
                    if idx >= 4:
                        v.wait_ge(b.sKT[idx % 4].h, vst_marks[idx - 4])
                    else:
                        v.wait_ge(b.sKT[idx % 4].h, kt_after_k[idx % 4])
                    v.scalar_tensor_tensor(out=kst[:, (idx % 4) * HALF:(idx % 4 + 1) * HALF],
                                           in0=ps[:, :], scalar=1.0,
                                           in1=bvb[:, blk * 512:(blk + 1) * 512],
                                           op0=ALU.mult, op1=ALU.add).then_inc(sDVE.h, 1)
                    b.sDVE.v += 1
                block.vector(vev)
                v_marks.append(b.sDVE.v)
                m_vev = b.sDVE.v

                skt = b.sKT[idx % 4]

                def vst(sy, blk=blk, tt=tt, idx=idx, m_vev=m_vev, skt=skt):
                    sy.wait_ge(sDVE.h, m_vev)
                    dst = b.kvb_V4[blk].ap()[tt * 128:(tt + 1) * 128, :]
                    sy.dma_start(out=dst,
                                 in_=kst[:, (idx % 4) * HALF:(idx % 4 + 1) * HALF]).then_inc(skt.h, 16)
                    skt.v += 16
                block.sync(vst)
                vst_marks.append(skt.v)
        m_v_done = b.sPE.v
        b.ccK_late_gate = [(b.sKT[i].h, kt_after_k[i]) for i in range(4)]
        b.ccV_late_gates = [
            [(b.sKT[i].h, vst_marks[blk * 8 + 4 + i]) for i in range(4)]
            for blk in range(1, 4)]

        if "kvb" in b.dbg_d:
            kvb_waits = [(s, s.v) for s in b.sKT]

            def dump_kvb(g):
                for ss, vv in kvb_waits:
                    g.wait_ge(ss.h, vv)
                for i, src_t in enumerate(b.kvb_K4):
                    g.dma_start(out=b.dbg_d["kvb"][i * 512:(i + 1) * 512, :],
                                in_=src_t.ap()).then_inc(b.sDBG.h, 16)
                    b.sDBG.v += 16
                for i, src_t in enumerate(b.kvb_V4):
                    dst = b.dbg_d["kvb"][2048 + i * 512:2048 + (i + 1) * 512, :] \
                        .rearrange("a (b c) -> (a b) c", c=512)
                    g.dma_start(out=dst, in_=src_t.ap()).then_inc(b.sDBG.h, 16)
                    b.sDBG.v += 16
            block.gpsimd(dump_kvb)

        q_evac, q_red, _, q_pans = proj("q", b.wqT, b.bq_sb, (b.nmA, b.nmB),
                                        [(sPE, vpan_marks[2]), (sPE, m_v_done)])
        b.m_q_done = b.sPE.v
        _dump(b, "q", q_all[:, :], sACT, q_evac[-1])

        def qfold(v):
            v.wait_ge(sDVE.h, q_red[-1])
            v.wait_ge(sDVE.h, b.sDVE.v)
            v.tensor_tensor(out=b.qm2[:, :], in0=b.nmA[:, 0:16],
                            in1=b.nmB[:, 0:16], op=ALU.max).then_inc(sDVE.h, 1)
            b.sDVE.v += 1
        block.vector(qfold)
        m_qm2 = b.sDVE.v

        def ldkn(sy):
            sy.wait_ge(sCC.h, b.m_cc2)
            sy.dma_start(out=b.rowe[:, :], in_=b.kn_g.ap()).then_inc(b.sKN.h, 16)
            b.sKN.v += 16
        block.sync(ldkn)
        m_kng = b.sKN.v

        def crow(v):
            v.wait_ge(b.sKN.h, m_kng)
            v.wait_ge(sDVE.h, m_qm2)
            v.tensor_tensor(out=b.rowe[:, :], in0=b.qm2[:, :], in1=b.rowe[:, :],
                            op=ALU.mult).then_inc(sDVE.h, 1)
            b.sDVE.v += 1
        block.vector(crow)
        m_crow = b.sDVE.v

        def csqrt(s):
            s.wait_ge(sDVE.h, m_crow)
            s.activation(out=b.rowe[:, :], in_=b.rowe[:, :], func=AF.Sqrt,
                         scale=1.0 / 128.0).then_inc(sACT.h, 1)
            b.sACT.v += 1
        block.scalar(csqrt)
        m_csq = b.sACT.v

        def cneg(v):
            v.wait_ge(sACT.h, m_csq)
            v.tensor_scalar(out=b.rowe[:, :], in0=b.rowe[:, :], scalar1=-1.0,
                            scalar2=None, op0=ALU.mult).then_inc(sDVE.h, 1)
            b.sDVE.v += 1
        block.vector(cneg)
        m_cneg = b.sDVE.v

        def cbc(t):
            t.wait_ge(sDVE.h, m_cneg)
            t.wait_ge(sACT.h, q_evac[-1])
            t.matmul(out=C[:, 0:16], lhsT=b.onerf[:, :], rhs=b.rowe[:, :],
                     start=True, stop=True).then_inc(sPE.h, 1)
            b.sPE.v += 1
        block.tensor(cbc)
        m_cbc = b.sPE.v

        def ccp(v):
            v.wait_ge(sPE.h, m_cbc)
            v.tensor_copy(out=b.negc[:, :], in_=C[:, 0:16]).then_inc(sDVE.h, 1)
            b.sDVE.v += 1
        block.vector(ccp)
        b.m_negc = b.sDVE.v
        _dump(b, "negc", b.negc[:, :], sDVE, b.m_negc)


ATT_G4 = ([3, 2] * 12 + [3, 1])
assert sum(ATT_G4) == 64


def _attn(b):
    nc, block = b.nc, b.block
    sPE, sACT, sDVE, sCC = b.sPE, b.sACT, b.sDVE, b.sCC
    q_all = b.q_all

    with ExitStack() as scm:
        e = scm.enter_context
        Kb = e(nc.sbuf_tensor("Kb", [128, 2 * 8192], BF16))
        Vb = e(nc.sbuf_tensor("Vb", [128, 2 * 8192], BF16))
        Pb = e(nc.sbuf_tensor("Pb", [128, 3 * 1536], BF16))
        Rg = e(nc.sbuf_tensor("Rg", [128, 1536], BF16))
        Rf = e(nc.sbuf_tensor("Rf", [128, 512], F32))
        drb = e(nc.sbuf_tensor("drb", [128, 512], F32))
        uu = e(nc.sbuf_tensor("uu", [128, 512], F32))
        spsA = e(nc.psum_tensor("spsA", [128, 1536], F32))
        spsB = e(nc.psum_tensor("spsB", [128, 1024], F32))
        ao = [e(nc.psum_tensor(f"ao{i}", [128, 512], F32)) for i in range(2)]
        ebp = e(nc.psum_tensor("ebp", [128, 512], F32))

        NG = len(ATT_G4)
        goff = []
        t0 = 0
        for sz in ATT_G4:
            goff.append((t0, sz))
            t0 += sz
        NIT = 32

        pe0 = b.sPE.v
        act0 = b.sACT.v
        dve0 = b.sDVE.v
        ACT_IT = NG
        DVE_IT = NG + 6

        sched = []
        for i in range(NIT):
            ops = [("S", 0), ("S", 1)]
            if i >= 1:
                ops.append(("dmm", i - 1))
            for g in range(NG):
                if g + 2 < NG:
                    ops.append(("S", g + 2))
                ops.append(("AO", g))
            if i >= 1:
                ops.insert(6, ("bc2", i - 1))
            sched.append(ops)
        sched.append([("dmm", NIT - 1), ("bc2", NIT - 1)])

        S_mark = [dict() for _ in range(NIT)]
        AO_mark = [dict() for _ in range(NIT)]
        DMM_mark = [None] * NIT
        BC2_mark = [None] * NIT
        blk_end = [None] * (NIT + 1)
        c = pe0
        for bi, ops in enumerate(sched):
            for kind, gg in ops:
                c += 1
                if kind == "S":
                    S_mark[bi][gg] = c
                elif kind == "AO":
                    AO_mark[bi][gg] = c
                elif kind == "dmm":
                    DMM_mark[gg] = c
                else:
                    BC2_mark[gg] = c
            blk_end[bi] = c
        expm = [[act0 + i * ACT_IT + g + 1 for g in range(NG)] for i in range(NIT)]
        raccm = [[dve0 + i * DVE_IT + g + 1 for g in range(NG)] for i in range(NIT)]
        foldm = [dve0 + i * DVE_IT + NG + 2 for i in range(NIT)]
        recm = [dve0 + i * DVE_IT + NG + 3 for i in range(NIT)]
        uum = [dve0 + i * DVE_IT + NG + 5 for i in range(NIT)]
        xm = [dve0 + i * DVE_IT + NG + 6 for i in range(NIT)]

        kv_marks = [None] * 16
        kl_marks = [None] * 16
        b.m_x1 = xm[:]

        def late_gathers(g):
            g.wait_ge(sPE.h, blk_end[0])
            for i in range(1, 4):
                for hh, vv in (b.ccK_late_gate if i == 1 else []):
                    g.wait_ge(hh, vv)
                g.collective_compute("AllGather", ALU.bypass,
                                     replica_groups=[list(range(R))],
                                     ins=[b.kvb_K4[i].ap().opt()],
                                     outs=[b.kvg_K4[i].ap().opt()]).then_inc(sCC.h, 1)
                b.sCC.v += 1
                b.m_ccK4[i] = b.sCC.v
                for hh, vv in b.ccV_late_gates[i - 1]:
                    g.wait_ge(hh, vv)
                g.collective_compute("AllGather", ALU.bypass,
                                     replica_groups=[list(range(R))],
                                     ins=[b.kvb_V4[i].ap().opt()],
                                     outs=[b.kvg_V4[i].ap().opt()]).then_inc(sCC.h, 1)
                b.sCC.v += 1
                b.m_ccV4[i] = b.sCC.v
        block.gpsimd(late_gathers)

        if "ffn" in b.phases:
            def stage_ffn(g):
                g.wait_ge(sPE.h, blk_end[1])
                for w_src, w_dst in [(b.gate_sl, b.g8b), (b.up_sl, b.u8b),
                                     (b.down_sl, b.d8b)]:
                    g.dma_start(out=w_dst.ap(), in_=w_src).then_inc(b.sF8.h, 16)
                    b.sF8.v += 16
                g.wait_ge(b.sF8.h, b.sF8.v)
                g.wait_ge(sPE.h, blk_end[11])
                for w8b, w8g in [(b.g8b, b.g8), (b.u8b, b.u8), (b.d8b, b.d8)]:
                    g.collective_compute("AllGather", ALU.bypass,
                                         replica_groups=[list(range(R))],
                                         ins=[w8b.ap().opt()],
                                         outs=[w8g.ap().opt()]).then_inc(sCC.h, 1)
                    b.sCC.v += 1
            block.gpsimd(stage_ffn)
            b.m_ccF8 = b.sCC.v

        for h in range(16):
            hb = h % 2
            skv = b.sKV[hb]
            skl = b.sKL[hb]

            def kload(sy, h=h, hb=hb, skl=skl):
                if h % 4 == 0:
                    sy.wait_ge(sCC.h, b.m_ccK4[h // 4])
                if h >= 2:
                    sy.wait_ge(sPE.h, blk_end[2 * h - 3])
                kg = b.kvg_K4[h // 4]
                for r in range(R):
                    sy.dma_start(
                        out=Kb[:, hb * 8192 + r * 1024:hb * 8192 + (r + 1) * 1024],
                        in_=kg.ap()[r * 512 + (h % 4) * 128:
                                    r * 512 + (h % 4 + 1) * 128, :]
                    ).then_inc(skl.h, 16)
                    skl.v += 16
            block.sync(kload)
            kl_marks[h] = skl.v

            def vload(sy, h=h, hb=hb, skv=skv):
                if h % 4 == 0:
                    sy.wait_ge(sCC.h, b.m_ccV4[h // 4])
                if h >= 2:
                    sy.wait_ge(sPE.h, blk_end[2 * h - 3])
                vg = b.kvg_V4[h // 4]
                for r in range(R):
                    base = (r * 1024) * 512 + (h % 4) * 128
                    src = bass.AP(vg, base, [[512, 128], [128 * 512, 8], [1, 128]])
                    dst = Vb[:, hb * 8192 + r * 1024:hb * 8192 + (r + 1) * 1024] \
                        .rearrange("p (tt d) -> p tt d", d=128)
                    sy.dma_start(out=dst, in_=src).then_inc(skv.h, 16)
                    skv.v += 16
            block.sync(vload)
            kv_marks[h] = skv.v

            for half in range(2):
                it = 2 * h + half
                qs = q_all[:, h * TOK + half * HALF: h * TOK + half * HALF + HALF]
                aop = ao[it % 2]

                def pe_sec(t, h=h, hb=hb, half=half, it=it, qs=qs, aop=aop):
                    for kind, gg in sched[it]:
                        if kind == "S":
                            g0, sz = goff[gg]
                            sps = spsA if gg % 2 == 0 else spsB
                            if gg == 0:
                                t.wait_ge(b.sKL[hb].h, kl_marks[h])
                                if it == 0:
                                    t.wait_ge(sDVE.h, b.m_negc)
                            if gg >= 2:
                                t.wait_ge(sACT.h, expm[it][gg - 2])
                            for ti in range(sz):
                                kt = g0 + ti
                                ins = t.matmul(out=sps[:, ti * 512:(ti + 1) * 512],
                                               lhsT=Kb[:, hb * 8192 + kt * 128:hb * 8192 + (kt + 1) * 128],
                                               rhs=qs, start=True, stop=True)
                            ins.then_inc(sPE.h, 1)
                            b.sPE.v += 1
                        elif kind == "AO":
                            g0, sz = goff[gg]
                            t.wait_ge(sACT.h, expm[it][gg])
                            if gg == 0:
                                t.wait_ge(b.sKV[hb].h, kv_marks[h])
                                if it >= 2:
                                    t.wait_ge(sDVE.h, uum[it - 2])
                            for ti in range(sz):
                                kt = g0 + ti
                                ins = t.matmul(out=aop[:, :],
                                               lhsT=Vb[:, hb * 8192 + kt * 128:hb * 8192 + (kt + 1) * 128],
                                               rhs=Pb[:, (gg % 3) * 1536 + ti * 512:(gg % 3) * 1536 + (ti + 1) * 512],
                                               start=(kt == 0), stop=(kt == 63))
                            ins.then_inc(sPE.h, 1)
                            b.sPE.v += 1
                        elif kind == "dmm":
                            t.wait_ge(sDVE.h, foldm[gg])
                            t.matmul(out=ebp[0:1, :], lhsT=b.onecf[:, :], rhs=Rf[:, :],
                                     start=True, stop=True).then_inc(sPE.h, 1)
                            b.sPE.v += 1
                        else:
                            t.wait_ge(sDVE.h, recm[gg])
                            t.matmul(out=ebp[:, :], lhsT=b.onerf[:, :], rhs=b.rowa[:, :],
                                     start=True, stop=True).then_inc(sPE.h, 1)
                            b.sPE.v += 1
                block.tensor(pe_sec)
                assert b.sPE.v == blk_end[it]

                def act_sec(s, h=h, it=it):
                    for gi in range(NG):
                        g0, sz = goff[gi]
                        sps = spsA if gi % 2 == 0 else spsB
                        s.wait_ge(sPE.h, S_mark[it][gi])
                        if gi >= 3:
                            s.wait_ge(sPE.h, AO_mark[it][gi - 3])
                            s.wait_ge(sDVE.h, raccm[it][gi - 3])
                        elif it > 0:
                            gstar = NG - 1 - ((NG - 1 - gi) % 3)
                            s.wait_ge(sPE.h, AO_mark[it - 1][gstar])
                            s.wait_ge(sDVE.h, raccm[it - 1][gstar])
                        s.activation(out=Pb[:, (gi % 3) * 1536:(gi % 3) * 1536 + sz * 512],
                                     in_=sps[:, 0:sz * 512], func=AF.Exp,
                                     bias=b.negc[:, h:h + 1],
                                     scale=SCALE).then_inc(sACT.h, 1)
                        b.sACT.v += 1
                block.scalar(act_sec)
                assert b.sACT.v == expm[it][NG - 1]

                def dve_sec(v, h=h, half=half, it=it, aop=aop):
                    for gi in range(NG):
                        g0, sz = goff[gi]
                        v.wait_ge(sACT.h, expm[it][gi])
                        seg = Pb[:, (gi % 3) * 1536:(gi % 3) * 1536 + sz * 512]
                        if gi == 0:
                            ins = v.tensor_copy(out=Rg[:, 0:sz * 512], in_=seg)
                        else:
                            v.wait_ge(sDVE.h, raccm[it][gi - 1])
                            ins = v.tensor_tensor(out=Rg[:, 0:sz * 512],
                                                  in0=Rg[:, 0:sz * 512],
                                                  in1=seg, op=ALU.add)
                        ins.then_inc(sDVE.h, 1)
                        b.sDVE.v += 1
                    v.wait_ge(sDVE.h, raccm[it][NG - 1])
                    v.tensor_tensor(out=Rf[:, :], in0=Rg[:, 0:512],
                                    in1=Rg[:, 512:1024], op=ALU.add).then_inc(sDVE.h, 1)
                    b.sDVE.v += 1
                    v.wait_ge(sDVE.h, b.sDVE.v)
                    v.tensor_tensor(out=Rf[:, :], in0=Rf[:, :],
                                    in1=Rg[:, 1024:1536], op=ALU.add).then_inc(sDVE.h, 1)
                    b.sDVE.v += 1
                    v.wait_ge(sPE.h, DMM_mark[it])
                    v.reciprocal(out=b.rowa[:, :], in_=ebp[0:1, :]).then_inc(sDVE.h, 1)
                    b.sDVE.v += 1
                    v.wait_ge(sPE.h, BC2_mark[it])
                    v.wait_ge(sPE.h, AO_mark[it][NG - 1])
                    v.tensor_copy(out=drb[:, :], in_=ebp[:, :]).then_inc(sDVE.h, 1)
                    b.sDVE.v += 1
                    v.wait_ge(sDVE.h, b.sDVE.v)
                    v.tensor_tensor(out=uu[:, :], in0=aop[:, :], in1=drb[:, :],
                                    op=ALU.mult).then_inc(sDVE.h, 1)
                    b.sDVE.v += 1
                    v.wait_ge(sDVE.h, b.sDVE.v)
                    xs = b.x_fm[:, h * TOK + half * HALF: h * TOK + half * HALF + HALF]
                    v.scalar_tensor_tensor(out=xs, in0=uu[:, :],
                                           scalar=b.gacol[:, h:h + 1], in1=xs,
                                           op0=ALU.mult, op1=ALU.add).then_inc(sDVE.h, 1)
                    b.sDVE.v += 1
                block.vector(dve_sec)
                assert b.sDVE.v == xm[it], (b.sDVE.v, xm[it])

        def pe_tail(t):
            t.wait_ge(sDVE.h, foldm[NIT - 1])
            t.matmul(out=ebp[0:1, :], lhsT=b.onecf[:, :], rhs=Rf[:, :],
                     start=True, stop=True).then_inc(sPE.h, 1)
            b.sPE.v += 1
            t.wait_ge(sDVE.h, recm[NIT - 1])
            t.matmul(out=ebp[:, :], lhsT=b.onerf[:, :], rhs=b.rowa[:, :],
                     start=True, stop=True).then_inc(sPE.h, 1)
            b.sPE.v += 1
        block.tensor(pe_tail)
        assert b.sPE.v == blk_end[NIT]
    b.m_attn_done = b.sDVE.v


def _ffn(b):
    nc, block = b.nc, b.block
    sPE, sACT, sDVE, sCC = b.sPE, b.sACT, b.sDVE, b.sCC

    with ExitStack() as scm:
        e = scm.enter_context
        h2 = e(nc.sbuf_tensor("h2_all", [128, NCH * TOK], BF16))
        fuse = e(nc.sbuf_tensor("fuse", [128, 48 * HALF], BF16))
        wpan = e(nc.sbuf_tensor("wpanF", [128, 32768], F8))
        g_sb = e(nc.sbuf_tensor("g_sb", [128, 2 * HALF], BF16))
        xbf = e(nc.sbuf_tensor("ln2_xbf", [128, 2 * HALF], BF16))
        xsq = e(nc.sbuf_tensor("ln2_xsq", [128, 2 * HALF], BF16))
        u_sb = e(nc.sbuf_tensor("ln2_u", [128, 2 * HALF], F32))
        A = e(nc.psum_tensor("f_A", [1, 512], F32))
        Bp = e(nc.psum_tensor("f_B", [1, 512], F32))
        C = e(nc.psum_tensor("f_C", [128, 512], F32))
        Dp = e(nc.psum_tensor("f_D", [128, 512], F32))
        Ep = e(nc.psum_tensor("f_E", [128, 512], F32))
        Fp = e(nc.psum_tensor("f_F", [128, 512], F32))
        Gp = e(nc.psum_tensor("f_G", [128, 512], F32))
        Hp = e(nc.psum_tensor("f_H", [128, 512], F32))

        sc = {"A": A, "B": Bp, "C": Gp, "D": Hp, "xbf": xbf, "xsq": xsq, "u": u_sb}
        h2m = _ln(b, sc, h2, s_off=32, c_off=48)
        b.m_h2 = b.sDVE.v
        _dump(b, "h2", h2[:, :], sDVE, b.m_h2)
        m_ln2_apply = sc["apply_done"][1]

        last_down_mark = {0: None, 1: None}
        down_subs = {0: None, 1: None}
        B_ROFF = [0, 16384, 8192, 24576]
        for half in range(2):
            ffnA_pe0 = b.sPE.v
            ffnA_act0 = b.sACT.v
            ffnA_dve0 = b.sDVE.v
            gmm_mark = [ffnA_pe0 + (i // 4) * 8 + (i % 4) * 2 + 1 for i in range(48)]
            umm_mark = [ffnA_pe0 + (i // 4) * 8 + (i % 4) * 2 + 2 for i in range(48)]
            evg_mark = [ffnA_act0 + i + 1 for i in range(48)]
            fus_mark = [ffnA_dve0 + i + 1 for i in range(48)]
            pan_marks = {}
            for sp in range(12):
                par = sp % 2
                sfp = b.sFP[par]

                def lda(sy, sp=sp, par=par, half=half, sfp=sfp):
                    if sp >= 2:
                        sy.wait_ge(sPE.h, pan_marks[sp - 2])
                    elif half == 1:
                        sy.wait_ge(sPE.h, down_subs[0][13 if sp == 0 else 15])
                    if sp < 2 and half == 0:
                        sy.wait_ge(b.sCC.h, b.m_ccF8)
                    for w, w8 in ((0, b.g8), (1, b.u8)):
                        for g4 in range(4):
                            src = bass.AP(w8, (g4 * 512) * FF + sp * 512,
                                          [[FF, 512], [1, 512]]) \
                                .rearrange("(ic p) o -> p ic o", p=128)
                            dst = wpan[:, w * 16384 + par * 8192 + g4 * 2048:
                                       w * 16384 + par * 8192 + (g4 + 1) * 2048] \
                                .rearrange("p (ic o) -> p ic o", o=512)
                            sy.dma_start(out=dst, in_=src).then_inc(sfp.h, 16)
                            sfp.v += 16
                block.sync(lda)
                m_pan = sfp.v

                def pe_sec(t, sp=sp, par=par, m_pan=m_pan, half=half, sfp=sfp):
                    for icq in range(4):
                        idx = sp * 4 + icq
                        gps = C if idx % 2 == 0 else Dp
                        ups = Ep if idx % 2 == 0 else Fp
                        t.wait_ge(sfp.h, m_pan)
                        if idx >= 2:
                            t.wait_ge(sACT.h, evg_mark[idx - 2])
                        for ic in range(NCH):
                            if idx == 0:
                                t.wait_ge(sDVE.h, h2m[half][ic])
                            ins = t.matmul(out=gps[:, :],
                                           lhsT=wpan[:, par * 8192 + ic * 512 + icq * 128:
                                                     par * 8192 + ic * 512 + (icq + 1) * 128],
                                           rhs=h2[:, ic * TOK + half * HALF:
                                                  ic * TOK + half * HALF + HALF],
                                           start=(ic == 0), stop=(ic == NCH - 1))
                        ins.then_inc(sPE.h, 1)
                        b.sPE.v += 1
                        if idx >= 2:
                            t.wait_ge(sDVE.h, fus_mark[idx - 2])
                        for ic in range(NCH):
                            ins = t.matmul(out=ups[:, :],
                                           lhsT=wpan[:, 16384 + par * 8192 + ic * 512 + icq * 128:
                                                     16384 + par * 8192 + ic * 512 + (icq + 1) * 128],
                                           rhs=h2[:, ic * TOK + half * HALF:
                                                  ic * TOK + half * HALF + HALF],
                                           start=(ic == 0), stop=(ic == NCH - 1))
                        ins.then_inc(sPE.h, 1)
                        b.sPE.v += 1
                block.tensor(pe_sec)
                assert b.sPE.v == umm_mark[sp * 4 + 3]
                pan_marks[sp] = b.sPE.v

                def act_sec(s, sp=sp):
                    for icq in range(4):
                        idx = sp * 4 + icq
                        gps = C if idx % 2 == 0 else Dp
                        s.wait_ge(sPE.h, gmm_mark[idx])
                        if idx >= 2:
                            s.wait_ge(sDVE.h, fus_mark[idx - 2])
                        s.activation(out=g_sb[:, (idx % 2) * HALF:(idx % 2 + 1) * HALF],
                                     in_=gps[:, :], func=AF.Silu,
                                     scale=b.gsc_sb[:, idx:idx + 1]).then_inc(sACT.h, 1)
                        b.sACT.v += 1
                block.scalar(act_sec)
                assert b.sACT.v == evg_mark[sp * 4 + 3]

                def dve_sec(v, sp=sp):
                    for icq in range(4):
                        idx = sp * 4 + icq
                        ups = Ep if idx % 2 == 0 else Fp
                        v.wait_ge(sPE.h, umm_mark[idx])
                        v.wait_ge(sACT.h, evg_mark[idx])
                        v.scalar_tensor_tensor(
                            out=fuse[:, idx * HALF:(idx + 1) * HALF],
                            in0=ups[:, :], scalar=b.usc_sb[:, idx:idx + 1],
                            in1=g_sb[:, (idx % 2) * HALF:(idx % 2 + 1) * HALF],
                            op0=ALU.mult, op1=ALU.mult).then_inc(sDVE.h, 1)
                        b.sDVE.v += 1
                block.vector(dve_sec)
                assert b.sDVE.v == fus_mark[sp * 4 + 3]
            m_fuse_all = fus_mark[-1]
            m_phaseA_pe = b.sPE.v

            dn_marks = []
            sub_mm_marks = []
            subi = 0
            for hc2 in range(8):
                c0 = hc2 * 256
                m_subs = []
                for ip in range(2):
                    r4 = subi % 4
                    sfp = b.sFPB[r4]
                    roff = B_ROFF[r4]

                    def ldd(sy, ip=ip, roff=roff, c0=c0, subi=subi, sfp=sfp):
                        if subi >= 4:
                            sy.wait_ge(sPE.h, sub_mm_marks[subi - 4])
                        elif subi < 2:
                            sy.wait_ge(sPE.h, pan_marks[10])
                        else:
                            sy.wait_ge(sPE.h, m_phaseA_pe)
                        for q4 in range(4):
                            src = bass.AP(b.d8, (ip * 3072 + q4 * 768) * D + c0,
                                          [[D, 768], [1, 256]]) \
                                .rearrange("(icl p) u -> p icl u", p=128)
                            dst = wpan[:, roff + 6 * q4 * 256:
                                       roff + (6 * q4 + 6) * 256] \
                                .rearrange("p (icl u) -> p icl u", u=256)
                            sy.dma_start(out=dst, in_=src).then_inc(sfp.h, 16)
                            sfp.v += 16
                    block.sync(ldd)
                    m_subs.append(sfp.v)
                    subi += 1

                pairs = [(C, Dp), (Ep, Fp), (Gp, Hp)]
                pj = hc2 % 3
                for ip in range(2):
                    r4 = (subi - 2 + ip) % 4
                    roff = B_ROFF[r4]
                    m_dpan = m_subs[ip]
                    sfp = b.sFPB[r4]

                    def dmmn(t, hc2=hc2, ip=ip, roff=roff, m_dpan=m_dpan,
                             half=half, sfp=sfp, pj=pj):
                        t.wait_ge(sfp.h, m_dpan)
                        if ip == 0:
                            t.wait_ge(sDVE.h, m_fuse_all)
                            if hc2 >= 3:
                                t.wait_ge(sDVE.h, dn_marks[2 * (hc2 - 3) + 1])
                        for icd in range(24):
                            icg = ip * 24 + icd
                            for hq in range(2):
                                ps = pairs[pj][hq]
                                ins = t.matmul(out=ps[:, :],
                                               lhsT=wpan[:, roff + icd * 256 + hq * 128:
                                                         roff + icd * 256 + (hq + 1) * 128],
                                               rhs=fuse[:, icg * HALF:(icg + 1) * HALF],
                                               start=(icg == 0), stop=(icg == 47))
                        ins.then_inc(sPE.h, 1)
                        b.sPE.v += 1
                    block.tensor(dmmn)
                    sub_mm_marks.append(b.sPE.v)
                m_dn = b.sPE.v

                for hq in range(2):
                    hc = hc2 * 2 + hq
                    ps = pairs[pj][hq]

                    def dupd(v, hc=hc, half=half, ps=ps, m_dn=m_dn):
                        v.wait_ge(sPE.h, m_dn)
                        xs = b.x_fm[:, hc * TOK + half * HALF: hc * TOK + half * HALF + HALF]
                        v.scalar_tensor_tensor(out=xs, in0=ps[:, :],
                                               scalar=b.gbds[:, hc:hc + 1], in1=xs,
                                               op0=ALU.mult, op1=ALU.add).then_inc(sDVE.h, 1)
                        b.sDVE.v += 1
                    block.vector(dupd)
                    dn_marks.append(b.sDVE.v)
                    m_upd = b.sDVE.v

                    def outst(sy, hc=hc, half=half, m_upd=m_upd):
                        sy.wait_ge(sDVE.h, m_upd)
                        sy.dma_start(
                            out=b.out_d[hc * 128:(hc + 1) * 128,
                                        half * HALF:(half + 1) * HALF],
                            in_=b.x_fm[:, hc * TOK + half * HALF:
                                       hc * TOK + half * HALF + HALF]).then_inc(b.sST.h, 16)
                        b.sST.v += 16
                    block.sync(outst)
            last_down_mark[half] = b.sPE.v
            down_subs[half] = sub_mm_marks


def shard_inputs(ins):
    x = np.asarray(ins["L0_attn_x"])[0]
    xT = np.ascontiguousarray(x.T)
    wqT = np.ascontiguousarray(np.asarray(ins["L0_attn_Wq"]).T)
    wkT = np.ascontiguousarray(np.asarray(ins["L0_attn_Wk"]).T)
    wvT = np.ascontiguousarray(np.asarray(ins["L0_attn_Wv"]).T)
    bqc = np.ascontiguousarray(np.asarray(ins["L0_attn_bq"]).reshape(16, 128).T)
    bkc = np.ascontiguousarray(np.asarray(ins["L0_attn_bk"]).reshape(16, 128).T)
    bvr = np.ascontiguousarray(np.asarray(ins["L0_attn_bv"]).reshape(1, 2048))
    gscc = np.ascontiguousarray(np.asarray(ins["L0_ffn_gate_scale"]).reshape(48, 128).T)
    uscc = np.ascontiguousarray(np.asarray(ins["L0_ffn_up_scale"]).reshape(48, 128).T)
    dscc = np.ascontiguousarray(np.asarray(ins["L0_ffn_down_scale"]).reshape(16, 128).T)
    cvec = np.ascontiguousarray(np.asarray(ins["L0_c"]).reshape(1, 2048))
    wmod = np.asarray(ins["L0_wmod"])
    gate_w = np.asarray(ins["L0_ffn_gate_w"])
    up_w = np.asarray(ins["L0_ffn_up_w"])
    down_w = np.asarray(ins["L0_ffn_down_w"])
    id_f32 = np.eye(128, dtype=np.float32)
    maps = []
    for c in range(R):
        maps.append({
            "xT": np.ascontiguousarray(xT[:, c * TOK:(c + 1) * TOK]),
            "wqT": wqT, "wkT": wkT, "wvT": wvT,
            "bqc": bqc, "bkc": bkc, "bvr": bvr,
            "gscc": gscc, "uscc": uscc, "dscc": dscc,
            "cvec": cvec,
            "wmod_sl": np.ascontiguousarray(wmod[:, c * 1536:(c + 1) * 1536]),
            "gate_sl": np.ascontiguousarray(gate_w[c * 256:(c + 1) * 256, :]),
            "up_sl": np.ascontiguousarray(up_w[c * 256:(c + 1) * 256, :]),
            "down_sl": np.ascontiguousarray(down_w[c * 768:(c + 1) * 768, :]),
            "id_f32": id_f32,
            "ones_colb": np.ones((128, 1), dtype=ml_dtypes.bfloat16),
            "ones_colf": np.ones((128, 1), dtype=np.float32),
            "ones_rowf": np.ones((1, 128), dtype=np.float32),
        })
    return maps


def kernel(**inputs):
    maps = shard_inputs(inputs)
    nc, _ = build()
    res = run_bass_kernel_spmd(nc, maps, core_ids=list(range(R)))
    parts = [np.asarray(r["out"]).T for r in res.results]
    return np.concatenate(parts, axis=0).reshape(1, L, D).astype(np.float32)


# version 15
# speedup vs baseline: 1.0328x; 1.0328x over previous
import math
from contextlib import ExitStack

import numpy as np
import ml_dtypes

import concourse.bass as bass
import concourse.mybir as mybir
from concourse.bass_utils import run_bass_kernel_spmd

F32 = mybir.dt.float32
BF16 = mybir.dt.bfloat16
F8 = mybir.dt.float8e4
AF = mybir.ActivationFunctionType
ALU = mybir.AluOpType
AX = mybir.AxisListType

D = 2048
L = 8192
TOK = 1024
NCH = 16
FF = 6144
R = 8
HALF = 512
SCALE = 1.0 / math.sqrt(128.0)
EPS = 1e-5
INV_D = 1.0 / D

ATT_GROUPS = [3] * 21 + [1]
ALL_PHASES = ("p1", "ln1", "qkv", "attn", "ffn")


class B:
    pass


def build(phases=ALL_PHASES, dbg=()):
    nc = bass.Bass("TRN2", target_bir_lowering=False)
    b = B()
    b.nc = nc
    b.phases = set(phases)
    b.dbg = set(dbg)

    din = lambda n, s, t: nc.dram_tensor(n, s, t, kind="ExternalInput").ap()
    b.xT = din("xT", [D, TOK], F32)
    b.wqT = din("wqT", [D, D], F32)
    b.wkT = din("wkT", [D, D], F32)
    b.wvT = din("wvT", [D, D], F32)
    b.bqc = din("bqc", [128, 16], F32)
    b.bkc = din("bkc", [128, 16], F32)
    b.bvr = din("bvr", [1, D], F32)
    b.gscc = din("gscc", [128, 48], F32)
    b.uscc = din("uscc", [128, 48], F32)
    b.dscc = din("dscc", [128, 16], F32)
    b.cvec = din("cvec", [1, D], F32)
    b.wmod_sl = din("wmod_sl", [D, 1536], F32)
    b.gate_sl = din("gate_sl", [D // R, FF], F32)
    b.up_sl = din("up_sl", [D // R, FF], F32)
    b.down_sl = din("down_sl", [FF // R, D], F32)
    b.id_f32 = din("id_f32", [128, 128], F32)
    b.ones_colb = din("ones_colb", [128, 1], BF16)
    b.ones_colf = din("ones_colf", [128, 1], F32)
    b.ones_rowf = din("ones_rowf", [1, 128], F32)

    b.out_d = nc.dram_tensor("out", [D, TOK], F32, kind="ExternalOutput").ap()

    b.dbg_d = {}
    for name, shape, dt in [
        ("modp", [128, 64], F32), ("gacol", [128, 16], F32), ("gbds", [128, 16], F32),
        ("h1", [128, NCH * TOK], BF16), ("q", [128, NCH * TOK], BF16),
        ("negc", [128, 16], F32), ("x1", [128, NCH * TOK], F32),
        ("h2", [128, NCH * TOK], BF16),
        ("kvb", [4096, TOK], BF16), ("kvg", [R * 4096, TOK], BF16),
    ]:
        if name in b.dbg:
            b.dbg_d[name] = nc.dram_tensor("dbg_" + name, shape, dt,
                                           kind="ExternalOutput").ap()

    b.mod_b = nc.dram_tensor("mod_b", [1, 1536], F32)
    b.mod_g = nc.dram_tensor("mod_g", [R, 1536], F32)
    b.kn_b = nc.dram_tensor("kn_b", [1, 16], F32)
    b.kn_g = nc.dram_tensor("kn_g", [1, 16], F32)
    b.kvb_K4 = [nc.dram_tensor(f"kvb_K4_{i}", [512, TOK], BF16) for i in range(4)]
    b.kvb_V4 = [nc.dram_tensor(f"kvb_V4_{i}", [1024, 512], BF16) for i in range(4)]
    b.kvg_K4 = [nc.dram_tensor(f"kvg_K4_{i}", [R * 512, TOK], BF16) for i in range(4)]
    b.kvg_V4 = [nc.dram_tensor(f"kvg_V4_{i}", [R * 1024, 512], BF16) for i in range(4)]
    b.g8b = nc.dram_tensor("g8b", [D // R, FF], F8)
    b.u8b = nc.dram_tensor("u8b", [D // R, FF], F8)
    b.d8b = nc.dram_tensor("d8b", [FF // R, D], F8)
    b.g8 = nc.dram_tensor("g8", [D, FF], F8)
    b.u8 = nc.dram_tensor("u8", [D, FF], F8)
    b.d8 = nc.dram_tensor("d8", [FF, D], F8)

    with ExitStack() as stack:
        e = stack.enter_context
        b.stack = stack
        b.x_fm = e(nc.sbuf_tensor("x_fm", [128, NCH * TOK], F32))
        b.id_sb = e(nc.sbuf_tensor("id_sb", [128, 128], F32))
        b.onecb = e(nc.sbuf_tensor("onecb", [128, 1], BF16))
        b.onecf = e(nc.sbuf_tensor("onecf", [128, 1], F32))
        b.onerf = e(nc.sbuf_tensor("onerf", [1, 128], F32))
        b.modp = e(nc.sbuf_tensor("modp", [128, 64], F32))
        b.gacol = e(nc.sbuf_tensor("gacol", [128, 16], F32))
        b.gbds = e(nc.sbuf_tensor("gbds", [128, 16], F32))
        b.bq_sb = e(nc.sbuf_tensor("bq_sb", [128, 16], F32))
        b.bk_sb = e(nc.sbuf_tensor("bk_sb", [128, 16], F32))
        b.gsc_sb = e(nc.sbuf_tensor("gsc_sb", [128, 48], F32))
        b.usc_sb = e(nc.sbuf_tensor("usc_sb", [128, 48], F32))
        b.dsc_sb = e(nc.sbuf_tensor("dsc_sb", [128, 16], F32))
        b.negc = e(nc.sbuf_tensor("negc", [128, 16], F32))
        b.qm2 = e(nc.sbuf_tensor("qm2", [1, 16], F32))
        b.nmA = e(nc.sbuf_tensor("nmA", [1, 16], F32))
        b.nmB = e(nc.sbuf_tensor("nmB", [1, 16], F32))
        b.rowa = e(nc.sbuf_tensor("rowa", [1, 512], F32))
        b.rowb = e(nc.sbuf_tensor("rowb", [1, 512], F32))
        b.rowc = e(nc.sbuf_tensor("rowc", [1, 512], F32))
        b.rowd = e(nc.sbuf_tensor("rowd", [1, 512], F32))
        b.rowe = e(nc.sbuf_tensor("rowe", [1, 16], F32))
        b.epsb = e(nc.sbuf_tensor("epsb", [128, 1], F32))
        b.cv_sb = e(nc.sbuf_tensor("cv_sb", [1, D], F32))

        class Sem:
            def __init__(self, name):
                self.h = e(nc.semaphore(name))
                self.v = 0
        b.Sem = Sem
        b.sPE = Sem("sPE")
        b.sACT = Sem("sACT")
        b.sDVE = Sem("sDVE")
        b.sCC = Sem("sCC")
        b.sST = Sem("sST")
        b.sBAR = Sem("sBAR")
        b.sGE = Sem("sGE")
        b.sDBG = Sem("sDBG")
        b.sCON = Sem("sCON")
        b.sXQ = [Sem(f"sXQ{i}") for i in range(4)]
        b.sWM = [Sem(f"sWM{i}") for i in range(2)]
        b.sF8 = Sem("sF8")
        b.sMS = Sem("sMS")
        b.sMG = Sem("sMG")
        b.sBV = Sem("sBV")
        b.sKS = Sem("sKS")
        b.sKN = Sem("sKN")
        b.sWP = [Sem(f"sWP{i}") for i in range(2)]
        b.sKT = [Sem(f"sKT{i}") for i in range(4)]
        b.sKV = [Sem(f"sKV{i}") for i in range(2)]
        b.sKL = [Sem(f"sKL{i}") for i in range(2)]
        b.sFP = [Sem(f"sFP{i}") for i in range(2)]
        b.sFPB = [Sem(f"sFPB{i}") for i in range(4)]
        b.all_dma_sems = ([b.sCON, b.sMS, b.sMG, b.sBV, b.sKS, b.sKN,
                           b.sST, b.sF8] + b.sXQ + b.sWM + b.sWP + b.sKT +
                          b.sKV + b.sKL + b.sFP + b.sFPB)

        b.block = e(nc.Block())
        _body(b)

        def fin(sy):
            sy.wait_ge(b.sST.h, b.sST.v)
        b.block.sync(fin)

        def fing(g):
            g.wait_ge(b.sDBG.h, b.sDBG.v)
        b.block.gpsimd(fing)
    return nc, b


def _bar(b, exclude=()):
    flush = [(s.h, s.v) for s in
             (b.all_dma_sems + [b.sPE, b.sACT, b.sDVE, b.sGE, b.sCC, b.sST, b.sDBG])
             if s.v > 0 and s not in exclude]
    for en in ("tensor", "scalar", "vector", "sync", "gpsimd"):
        def emit(eng):
            for hh, vv in flush:
                eng.wait_ge(hh, vv)
        getattr(b.block, en)(emit)


def _dump(b, name, src, sem, val):
    if name not in b.dbg_d:
        return
    dst = b.dbg_d[name]
    waits = sem if isinstance(sem, list) else [(sem, val)]

    def emit(g):
        for ss, vv in waits:
            g.wait_ge(ss.h, vv)
        g.dma_start(out=dst, in_=src).then_inc(b.sDBG.h, 16)
        b.sDBG.v += 16
    b.block.gpsimd(emit)


def _body(b):
    nc, block = b.nc, b.block

    def p0(sy):
        for dst, src in [(b.id_sb, b.id_f32), (b.onecb, b.ones_colb),
                         (b.onecf, b.ones_colf), (b.onerf, b.ones_rowf),
                         (b.bq_sb, b.bqc), (b.bk_sb, b.bkc),
                         (b.gsc_sb, b.gscc), (b.usc_sb, b.uscc),
                         (b.dsc_sb, b.dscc)]:
            sy.dma_start(out=dst[:, :], in_=src).then_inc(b.sCON.h, 16)
            b.sCON.v += 16
    block.sync(p0)

    def p0c(sy):
        sy.dma_start(out=b.cv_sb[:, :], in_=b.cvec).then_inc(b.sCON.h, 16)
        b.sCON.v += 16
    if "p1" in b.phases:
        block.sync(p0c)
    b.m_consts = b.sCON.v

    def p0e(v):
        v.memset(b.epsb[:, :], EPS).then_inc(b.sDVE.h, 1)
        b.sDVE.v += 1
    block.vector(p0e)
    b.m_eps = b.sDVE.v

    b.m_mod_ready = None
    if "p1" in b.phases:
        with ExitStack() as p1_scope:
            b.p1_scope = p1_scope
            _mod(b)
        _bar(b, exclude=tuple([b.sF8, b.sCC] + b.sXQ))

    def p0x(sy):
        for ic in range(NCH):
            sy.dma_start(out=b.x_fm[:, ic * TOK:(ic + 1) * TOK],
                         in_=b.xT[ic * 128:(ic + 1) * 128, :]).then_inc(
                             b.sXQ[ic // 4].h, 16)
            b.sXQ[ic // 4].v += 16
    block.sync(p0x)
    b.m_x_quarters = [s4.v for s4 in b.sXQ]

    if "ln1" in b.phases or "qkv" in b.phases:
        with nc.sbuf_tensor("q_all", [128, NCH * TOK], BF16) as q_all:
            b.q_all = q_all
            _qkv(b)
            _bar(b, exclude=(b.sCC,))
            if "attn" in b.phases:
                _attn(b)
                _bar(b)
        _dump(b, "x1", b.x_fm[:, :], b.sDVE, b.sDVE.v)

    if "ffn" in b.phases:
        _ffn(b)
    else:
        _store_x(b)


def _store_x(b):
    def emit(sy):
        for s4 in b.sXQ:
            sy.wait_ge(s4.h, s4.v)
        sy.wait_ge(b.sDVE.h, b.sDVE.v)
        for ic in range(NCH):
            sy.dma_start(out=b.out_d[ic * 128:(ic + 1) * 128, :],
                         in_=b.x_fm[:, ic * TOK:(ic + 1) * TOK]).then_inc(b.sST.h, 16)
            b.sST.v += 16
    b.block.sync(emit)


def _mod(b):
    nc, block = b.nc, b.block
    sPE, sACT, sDVE, sCC = b.sPE, b.sACT, b.sDVE, b.sCC
    cv_sb = b.cv_sb

    if True:
        e = b.p1_scope.enter_context
        sc_sb = e(nc.sbuf_tensor("sc_sb", [1, D], F32))
        scT = e(nc.sbuf_tensor("scT", [128, 16], F32))
        wm_sb = e(nc.sbuf_tensor("wm_sb", [128, 2 * 1536], F32))
        mod_sb = e(nc.sbuf_tensor("mod_sb", [1, 1536], F32))
        scT_ps = e(nc.psum_tensor("scT_ps", [128, 16], F32))
        mod_ps = e(nc.psum_tensor("mod_ps", [1, 1536], F32))

        def silu(s):
            s.wait_ge(b.sCON.h, b.m_consts)
            s.activation(out=sc_sb[:, :], in_=cv_sb[:, :], func=AF.Silu).then_inc(sACT.h, 1)
            b.sACT.v += 1
        block.scalar(silu)
        m_silu = b.sACT.v

        def tr16(t):
            t.wait_ge(sACT.h, m_silu)
            t.wait_ge(b.sCON.h, b.m_consts)
            for k in range(16):
                ins = t.transpose(out=scT_ps[:, k:k + 1],
                                  in_=sc_sb[:, k * 128:(k + 1) * 128],
                                  identity=b.id_sb[0:1, 0:1])
            ins.then_inc(sPE.h, 1)
            b.sPE.v += 1
        block.tensor(tr16)
        m_tr = b.sPE.v

        def cp(v):
            v.wait_ge(sPE.h, m_tr)
            v.tensor_copy(out=scT[:, :], in_=scT_ps[:, :]).then_inc(sDVE.h, 1)
            b.sDVE.v += 1
        block.vector(cp)
        m_scT = b.sDVE.v

        wm_marks = []
        for ic in range(NCH):
            par = ic % 2

            swm = b.sWM[par]

            def ldw(sy, ic=ic, par=par, swm=swm):
                if ic >= 2:
                    sy.wait_ge(sPE.h, wm_marks[ic - 2])
                sy.dma_start(out=wm_sb[:, par * 1536:(par + 1) * 1536],
                             in_=b.wmod_sl[ic * 128:(ic + 1) * 128, :]).then_inc(swm.h, 16)
                swm.v += 16
            block.sync(ldw)
            m_ldw = swm.v

            def mm(t, ic=ic, par=par, m_ldw=m_ldw, swm=swm):
                t.wait_ge(swm.h, m_ldw)
                if ic == 0:
                    t.wait_ge(sDVE.h, m_scT)
                for s3 in range(3):
                    ins = t.matmul(out=mod_ps[:, s3 * 512:(s3 + 1) * 512],
                                   lhsT=scT[:, ic:ic + 1],
                                   rhs=wm_sb[:, par * 1536 + s3 * 512:par * 1536 + (s3 + 1) * 512],
                                   start=(ic == 0), stop=(ic == NCH - 1))
                ins.then_inc(sPE.h, 1)
                b.sPE.v += 1
            block.tensor(mm)
            wm_marks.append(b.sPE.v)
        m_mm = b.sPE.v

        def evac(v):
            v.wait_ge(sPE.h, m_mm)
            v.tensor_copy(out=mod_sb[:, :], in_=mod_ps[:, :]).then_inc(sDVE.h, 1)
            b.sDVE.v += 1
        block.vector(evac)
        m_msb = b.sDVE.v

        def cc1(g):
            g.wait_ge(sDVE.h, m_msb)
            g.dma_start(out=b.mod_b.ap(), in_=mod_sb[:, :]).then_inc(b.sMS.h, 16)
            b.sMS.v += 16
            g.wait_ge(b.sMS.h, b.sMS.v)
            g.collective_compute("AllGather", ALU.bypass,
                                 replica_groups=[list(range(R))],
                                 ins=[b.mod_b.ap().opt()],
                                 outs=[b.mod_g.ap().opt()]).then_inc(sCC.h, 1)
            b.sCC.v += 1
        block.gpsimd(cc1)
        b.m_cc1 = b.sCC.v



def _ln(b, sc, dst_all, s_off, c_off, x_wait=None, x_chunks=None):
    nc, block = b.nc, b.block
    sPE, sACT, sDVE = b.sPE, b.sACT, b.sDVE
    A, Bp, C, Dp = sc["A"], sc["B"], sc["C"], sc["D"]
    xbf, xsq, u_sb = sc["xbf"], sc["xsq"], sc["u"]
    h_marks = [[None] * NCH, [None] * NCH]
    rows_done = {}
    apply_done = {}
    for half in range(2):
        peb = b.sPE.v
        actb = b.sACT.v
        pe_marks = [peb + ic + 1 for ic in range(NCH)]
        act_marks = [actb + ic + 1 for ic in range(NCH)]

        def act_sec(s, half=half, pe_marks=pe_marks):
            for ic in range(NCH):
                if x_chunks is not None and half == 0:
                    s.wait_ge(b.sXQ[ic // 4].h, x_chunks[ic // 4])
                elif x_wait is not None and ic == 0 and half == 0:
                    s.wait_ge(x_wait[0].h, x_wait[1])
                if ic >= 2:
                    s.wait_ge(sPE.h, pe_marks[ic - 2])
                src = b.x_fm[:, ic * TOK + half * HALF: ic * TOK + half * HALF + HALF]
                s.activation(out=xbf[:, (ic % 2) * HALF:(ic % 2 + 1) * HALF],
                             in_=src, func=AF.Identity)
                s.activation(out=xsq[:, (ic % 2) * HALF:(ic % 2 + 1) * HALF],
                             in_=src, func=AF.Square).then_inc(sACT.h, 1)
                b.sACT.v += 1
        block.scalar(act_sec)

        def pe_sec(t, half=half, act_marks=act_marks):
            for ic in range(NCH):
                t.wait_ge(sACT.h, act_marks[ic])
                if ic == 0 and half == 0:
                    t.wait_ge(b.sCON.h, b.m_consts)
                if ic == 0 and half == 1:
                    t.wait_ge(sDVE.h, rows_done[0])
                t.matmul(out=A[:, :], lhsT=b.onecb[:, :],
                         rhs=xbf[:, (ic % 2) * HALF:(ic % 2 + 1) * HALF],
                         start=(ic == 0), stop=(ic == NCH - 1))
                t.matmul(out=Bp[:, :], lhsT=b.onecb[:, :],
                         rhs=xsq[:, (ic % 2) * HALF:(ic % 2 + 1) * HALF],
                         start=(ic == 0), stop=(ic == NCH - 1)).then_inc(sPE.h, 1)
                b.sPE.v += 1
        block.tensor(pe_sec)
        assert b.sPE.v == pe_marks[-1]
        m_stats = b.sPE.v

        def rows(v):
            v.wait_ge(sPE.h, m_stats)
            v.tensor_scalar(out=b.rowa[:, :], in0=A[:, :], scalar1=INV_D,
                            scalar2=None, op0=ALU.mult)
            v.tensor_scalar(out=b.rowb[:, :], in0=Bp[:, :], scalar1=INV_D,
                            scalar2=None, op0=ALU.mult).then_inc(sDVE.h, 1)
            b.sDVE.v += 1
            v.wait_ge(sDVE.h, b.sDVE.v)
            v.tensor_tensor(out=b.rowc[:, :], in0=b.rowa[:, :], in1=b.rowa[:, :],
                            op=ALU.mult).then_inc(sDVE.h, 1)
            b.sDVE.v += 1
            v.wait_ge(sDVE.h, b.sDVE.v)
            v.tensor_tensor(out=b.rowc[:, :], in0=b.rowb[:, :], in1=b.rowc[:, :],
                            op=ALU.subtract).then_inc(sDVE.h, 1)
            b.sDVE.v += 1
        block.vector(rows)
        m_var = b.sDVE.v

        def rstd(s):
            s.wait_ge(sDVE.h, m_var)
            s.activation(out=b.rowd[:, :], in_=b.rowc[:, :], func=AF.Ln,
                         bias=b.epsb[0:1, :], scale=1.0).then_inc(sACT.h, 1)
            b.sACT.v += 1
            s.wait_ge(sACT.h, b.sACT.v)
            s.activation(out=b.rowd[:, :], in_=b.rowd[:, :], func=AF.Exp,
                         scale=-0.5).then_inc(sACT.h, 1)
            b.sACT.v += 1
        block.scalar(rstd)
        m_rstd = b.sACT.v

        def r2(v):
            v.wait_ge(sACT.h, m_rstd)
            v.scalar_tensor_tensor(out=b.rowc[:, :], in0=b.rowa[:, :], scalar=-1.0,
                                   in1=b.rowd[:, :], op0=ALU.mult,
                                   op1=ALU.mult).then_inc(sDVE.h, 1)
            b.sDVE.v += 1
        block.vector(r2)
        rows_done[half] = b.sDVE.v
        m_r2 = b.sDVE.v

        def bc(t, half=half):
            t.wait_ge(sDVE.h, m_r2)
            if half == 1:
                t.wait_ge(sDVE.h, apply_done[0])
            t.matmul(out=C[:, :], lhsT=b.onerf[:, :], rhs=b.rowd[:, :],
                     start=True, stop=True)
            t.matmul(out=Dp[:, :], lhsT=b.onerf[:, :], rhs=b.rowc[:, :],
                     start=True, stop=True).then_inc(sPE.h, 1)
            b.sPE.v += 1
        block.tensor(bc)
        m_bc = b.sPE.v

        def apply(v, half=half):
            v.wait_ge(sPE.h, m_bc)
            for ic in range(NCH):
                src = b.x_fm[:, ic * TOK + half * HALF: ic * TOK + half * HALF + HALF]
                dst = dst_all[:, ic * TOK + half * HALF: ic * TOK + half * HALF + HALF]
                up = u_sb[:, (ic % 2) * HALF:(ic % 2 + 1) * HALF]
                v.tensor_tensor(out=up, in0=src, in1=C[:, :], op=ALU.mult).then_inc(sDVE.h, 1)
                b.sDVE.v += 1
                v.wait_ge(sDVE.h, b.sDVE.v)
                v.tensor_tensor(out=up, in0=up, in1=Dp[:, :], op=ALU.add).then_inc(sDVE.h, 1)
                b.sDVE.v += 1
                v.wait_ge(sDVE.h, b.sDVE.v)
                v.tensor_scalar(out=dst, in0=up,
                                scalar1=b.modp[:, c_off + ic:c_off + ic + 1],
                                scalar2=b.modp[:, s_off + ic:s_off + ic + 1],
                                op0=ALU.mult, op1=ALU.add).then_inc(sDVE.h, 1)
                b.sDVE.v += 1
                h_marks[half][ic] = b.sDVE.v
        block.vector(apply)
        apply_done[half] = b.sDVE.v
    sc["apply_done"] = apply_done
    return h_marks


def _qkv(b):
    nc, block = b.nc, b.block
    sPE, sACT, sDVE, sCC = b.sPE, b.sACT, b.sDVE, b.sCC
    q_all = b.q_all

    with ExitStack() as scm:
        e = scm.enter_context
        h_all = e(nc.sbuf_tensor("h_all", [128, NCH * TOK], BF16))
        xbf = e(nc.sbuf_tensor("ln_xbf", [128, 2 * HALF], BF16))
        xsq = e(nc.sbuf_tensor("ln_xsq", [128, 2 * HALF], BF16))
        u_sb = e(nc.sbuf_tensor("ln_u", [128, 2 * HALF], F32))
        wp = e(nc.sbuf_tensor("wp", [128, 2 * 8192], BF16))
        kst = e(nc.sbuf_tensor("kst", [128, 4 * HALF], BF16))
        nsq = e(nc.sbuf_tensor("nsq", [128, 2 * HALF], BF16))
        bvb = e(nc.sbuf_tensor("bvb", [128, D], F32))
        bvrow = b.cv_sb
        A = e(nc.psum_tensor("ps_A", [1, 512], F32))
        Bp = e(nc.psum_tensor("ps_B", [1, 512], F32))
        C = e(nc.psum_tensor("ps_C", [128, 512], F32))
        Dp = e(nc.psum_tensor("ps_D", [128, 512], F32))
        Ep = e(nc.psum_tensor("ps_E", [128, 512], F32))
        Fp = e(nc.psum_tensor("ps_F", [128, 512], F32))
        Gp = e(nc.psum_tensor("ps_G", [128, 512], F32))
        Hp = e(nc.psum_tensor("ps_H", [128, 512], F32))

        def ld_w_panel(wT, pan, gate=None):
            par = pan % 2
            swp = b.sWP[par]

            def ld(g, wT=wT, pan=pan, par=par, swp=swp, gate=gate):
                if gate is not None:
                    g.wait_ge(gate[0].h, gate[1])
                for g4 in range(4):
                    src = wT.rearrange("(g ic p) o -> g p ic o", ic=4, p=128)[
                        g4, :, :, pan * 512:(pan + 1) * 512]
                    dst = wp[:, par * 8192 + g4 * 2048: par * 8192 + (g4 + 1) * 2048] \
                        .rearrange("p (ic o) -> p ic o", o=512)
                    g.dma_start(out=dst, in_=src).then_inc(swp.h, 16)
                    swp.v += 16
            block.gpsimd(ld)
            return swp.v

        if "qkv" in b.phases:
            k_pre = {0: ld_w_panel(b.wkT, 0), 1: ld_w_panel(b.wkT, 1)}

        mg_sb = e(nc.sbuf_tensor("mg_sb", [96, 128], F32))
        mT_sb = e(nc.sbuf_tensor("mT_sb", [128, 96], F32))
        sgn_sb = e(nc.sbuf_tensor("sgn_sb", [128, 32], F32))
        mag_sb = e(nc.sbuf_tensor("mag_sb", [128, 32], F32))

        def ldmg(sy):
            sy.wait_ge(sCC.h, b.m_cc1)
            sy.dma_start(out=mg_sb[:, :],
                         in_=b.mod_g.ap().rearrange("r (a c) -> (r a) c", c=128)).then_inc(b.sMG.h, 16)
            b.sMG.v += 16
        block.sync(ldmg)
        m_mg = b.sMG.v

        def trm(t):
            t.wait_ge(b.sMG.h, m_mg)
            t.transpose(out=C[:, 0:96], in_=mg_sb[:, :],
                        identity=b.id_sb[0:96, 0:96]).then_inc(sPE.h, 1)
            b.sPE.v += 1
        block.tensor(trm)
        m_trm = b.sPE.v

        def cpm(v):
            v.wait_ge(sPE.h, m_trm)
            v.tensor_copy(out=mT_sb[:, :], in_=C[:, 0:96]).then_inc(sDVE.h, 1)
            b.sDVE.v += 1
        block.vector(cpm)
        m_mT = b.sDVE.v

        def ge(g):
            g.wait_ge(sDVE.h, m_mT)
            g.tensor_scalar(out=sgn_sb[:, 0:16], in0=mT_sb[:, 32:48],
                            scalar1=0.0, scalar2=None, op0=ALU.is_ge)
            g.tensor_scalar(out=sgn_sb[:, 16:32], in0=mT_sb[:, 80:96],
                            scalar1=0.0, scalar2=None, op0=ALU.is_ge).then_inc(b.sGE.h, 1)
            b.sGE.v += 1
        block.gpsimd(ge)
        m_ge = b.sGE.v

        def ab(s):
            s.wait_ge(sDVE.h, m_mT)
            s.activation(out=mag_sb[:, 0:16], in_=mT_sb[:, 32:48], func=AF.Abs)
            s.activation(out=mag_sb[:, 16:32], in_=mT_sb[:, 80:96], func=AF.Abs).then_inc(sACT.h, 1)
            b.sACT.v += 1
        block.scalar(ab)
        m_abs = b.sACT.v

        def fold(v):
            v.wait_ge(b.sGE.h, m_ge)
            v.wait_ge(sACT.h, m_abs)
            v.wait_ge(b.sCON.h, b.m_consts)
            v.tensor_scalar(out=sgn_sb[:, 0:32], in0=sgn_sb[:, 0:32],
                            scalar1=2.0, scalar2=-1.0, op0=ALU.mult, op1=ALU.add)
            v.tensor_scalar(out=mag_sb[:, 0:32], in0=mag_sb[:, 0:32],
                            scalar1=0.25, scalar2=None, op0=ALU.max).then_inc(sDVE.h, 1)
            b.sDVE.v += 1
            v.wait_ge(sDVE.h, b.sDVE.v)
            v.tensor_tensor(out=b.gacol[:, :], in0=sgn_sb[:, 0:16],
                            in1=mag_sb[:, 0:16], op=ALU.mult)
            v.tensor_tensor(out=b.gbds[:, :], in0=sgn_sb[:, 16:32],
                            in1=mag_sb[:, 16:32], op=ALU.mult).then_inc(sDVE.h, 1)
            b.sDVE.v += 1
            v.wait_ge(sDVE.h, b.sDVE.v)
            v.tensor_tensor(out=b.gbds[:, :], in0=b.gbds[:, :],
                            in1=b.dsc_sb[:, :], op=ALU.mult)
            v.tensor_copy(out=b.modp[:, 0:16], in_=mT_sb[:, 0:16])
            v.tensor_scalar(out=b.modp[:, 16:32], in0=mT_sb[:, 16:32],
                            scalar1=1.0, scalar2=None, op0=ALU.add)
            v.tensor_copy(out=b.modp[:, 32:48], in_=mT_sb[:, 48:64])
            v.tensor_scalar(out=b.modp[:, 48:64], in0=mT_sb[:, 64:80],
                            scalar1=1.0, scalar2=None, op0=ALU.add).then_inc(sDVE.h, 1)
            b.sDVE.v += 1
        block.vector(fold)
        b.m_mod_ready = b.sDVE.v

        _dump(b, "modp", b.modp[:, :], sDVE, b.m_mod_ready)
        _dump(b, "gacol", b.gacol[:, :], sDVE, b.m_mod_ready)
        _dump(b, "gbds", b.gbds[:, :], sDVE, b.m_mod_ready)

        sc = {"A": A, "B": Bp, "C": Gp, "D": Hp, "xbf": xbf, "xsq": xsq, "u": u_sb}
        h1m = _ln(b, sc, h_all, s_off=0, c_off=16, x_chunks=b.m_x_quarters)
        b.m_h1 = b.sDVE.v
        _dump(b, "h1", h_all[:, :], sDVE, b.m_h1)

        if "qkv" not in b.phases:
            def drain(v):
                v.wait_ge(sDVE.h, b.sDVE.v)
            block.vector(drain)
            return

        m_ln1_apply = sc["apply_done"][1]

        def ldbv(sy):
            sy.dma_start(out=bvrow[:, :], in_=b.bvr).then_inc(b.sBV.h, 16)
            b.sBV.v += 16
        block.sync(ldbv)
        m_bvr = b.sBV.v

        bvb_marks = []
        for s4 in range(4):
            def bvmm(t, s4=s4):
                t.wait_ge(b.sBV.h, m_bvr)
                t.wait_ge(b.sCON.h, b.m_consts)
                if s4 > 0:
                    t.wait_ge(sDVE.h, bvb_marks[s4 - 1])
                t.matmul(out=Ep[:, :], lhsT=b.onerf[:, :],
                         rhs=bvrow[:, s4 * 512:(s4 + 1) * 512],
                         start=True, stop=True).then_inc(sPE.h, 1)
                b.sPE.v += 1
            block.tensor(bvmm)
            m_mmv = b.sPE.v

            def bvcp(v, s4=s4, m_mmv=m_mmv):
                v.wait_ge(sPE.h, m_mmv)
                v.tensor_copy(out=bvb[:, s4 * 512:(s4 + 1) * 512],
                              in_=Ep[:, :]).then_inc(sDVE.h, 1)
                b.sDVE.v += 1
            block.vector(bvcp)
            bvb_marks.append(b.sDVE.v)
        m_bvb = b.sDVE.v

        def proj(which, wT, bias_sb, norm_dst, pan_gates, pan_hooks=None,
                 st_box=None, pre_pans=None, h_chunk_marks=None):
            evac_marks = []
            sq_marks = []
            nmm_marks = []
            red_marks = []
            st_marks = []
            pan_marks = {}
            proj_dve0 = b.sDVE.v
            if which == "k":
                kt0 = [s.v for s in b.sKT]
                for idx in range(32):
                    st_marks.append(kt0[idx % 4] + 16 * (idx // 4 + 1))
                if st_box is not None:
                    st_box.append(st_marks)
            for pan in range(4):
                par = pan % 2
                swp = b.sWP[par]
                if pan_hooks and pan in pan_hooks:
                    pan_hooks[pan]()
                if pre_pans and pan in pre_pans:
                    m_pan = pre_pans[pan]
                elif pan >= 2:
                    m_pan = ld_w_panel(wT, pan, (sPE, pan_marks[pan - 2]))
                else:
                    m_pan = ld_w_panel(wT, pan,
                                       pan_gates[pan] if pan_gates else None)

                pan_pe0 = b.sPE.v
                pan_act0 = b.sACT.v
                pe_order = [("m", 0), ("m", 1)]
                for i in range(8):
                    if i + 2 < 8:
                        pe_order.append(("m", i + 2))
                    pe_order.append(("n", i))
                mmz_marks = [None] * 8
                nmm_loc = [None] * 8
                for pos, (kind, i) in enumerate(pe_order):
                    if kind == "m":
                        mmz_marks[i] = pan_pe0 + pos + 1
                    else:
                        nmm_loc[i] = pan_pe0 + pos + 1
                for i in range(8):
                    evac_marks.append(pan_act0 + 2 * i + 1)
                    sq_marks.append(pan_act0 + 2 * i + 2)
                    red_marks.append(proj_dve0 + pan * 8 + i + 1)
                    nmm_marks.append(nmm_loc[i])

                def pe_sec(t, pan=pan, par=par, m_pan=m_pan, swp=swp,
                           pe_order=pe_order):
                    for kind, i in pe_order:
                        idx = pan * 8 + i
                        if kind == "m":
                            dcq, half = i // 2, i % 2
                            ps = C if idx % 2 == 0 else Dp
                            t.wait_ge(swp.h, m_pan)
                            if idx == 0 and h_chunk_marks is None:
                                t.wait_ge(sDVE.h, m_ln1_apply)
                            if idx >= 2:
                                t.wait_ge(sACT.h, evac_marks[idx - 2])
                            for ic in range(NCH):
                                if idx <= 1 and h_chunk_marks is not None:
                                    t.wait_ge(sDVE.h, h_chunk_marks[half][ic])
                                ins = t.matmul(out=ps[:, :],
                                               lhsT=wp[:, par * 8192 + ic * 512 + dcq * 128:
                                                       par * 8192 + ic * 512 + (dcq + 1) * 128],
                                               rhs=h_all[:, ic * TOK + half * HALF:
                                                         ic * TOK + half * HALF + HALF],
                                               start=(ic == 0), stop=(ic == NCH - 1))
                            ins.then_inc(sPE.h, 1)
                            b.sPE.v += 1
                        else:
                            t.wait_ge(sACT.h, sq_marks[idx])
                            if idx >= 1:
                                t.wait_ge(sDVE.h, red_marks[idx - 1])
                            t.matmul(out=A[:, :], lhsT=b.onecb[:, :],
                                     rhs=nsq[:, (idx % 2) * HALF:(idx % 2 + 1) * HALF],
                                     start=True, stop=True).then_inc(sPE.h, 1)
                            b.sPE.v += 1
                block.tensor(pe_sec)
                assert b.sPE.v == pan_pe0 + 16
                pan_marks[pan] = b.sPE.v

                def act_sec(s, pan=pan, par=par, which=which, bias_sb=bias_sb,
                            mmz_marks=mmz_marks):
                    for i in range(8):
                        idx = pan * 8 + i
                        dcq, half = i // 2, i % 2
                        dc = pan * 4 + dcq
                        ps = C if idx % 2 == 0 else Dp
                        s.wait_ge(sPE.h, mmz_marks[i])
                        if which == "k":
                            if idx >= 4:
                                s.wait_ge(b.sKT[idx % 4].h, st_marks[idx - 4])
                            dst = kst[:, (idx % 4) * HALF:(idx % 4 + 1) * HALF]
                        else:
                            dst = q_all[:, dc * TOK + half * HALF:
                                        dc * TOK + half * HALF + HALF]
                        s.activation(out=dst, in_=ps[:, :], func=AF.Identity,
                                     bias=bias_sb[:, dc:dc + 1]).then_inc(sACT.h, 1)
                        b.sACT.v += 1
                        s.wait_ge(sACT.h, b.sACT.v)
                        if idx >= 2:
                            s.wait_ge(sPE.h, nmm_marks[idx - 2])
                        s.activation(out=nsq[:, (idx % 2) * HALF:(idx % 2 + 1) * HALF],
                                     in_=dst, func=AF.Square).then_inc(sACT.h, 1)
                        b.sACT.v += 1
                block.scalar(act_sec)
                assert b.sACT.v == sq_marks[-1]

                if which == "k":
                    def gp_st(sy, pan=pan):
                        for i in range(8):
                            idx = pan * 8 + i
                            dcq, half = i // 2, i % 2
                            dc = pan * 4 + dcq
                            skt = b.sKT[idx % 4]
                            sy.wait_ge(sACT.h, evac_marks[idx])
                            sy.dma_start(
                                out=b.kvb_K4[dc // 4].ap()[(dc % 4) * 128:(dc % 4 + 1) * 128,
                                                           half * HALF:(half + 1) * HALF],
                                in_=kst[:, (idx % 4) * HALF:(idx % 4 + 1) * HALF]
                            ).then_inc(skt.h, 16)
                            skt.v += 16
                            assert skt.v == st_marks[idx]
                    block.sync(gp_st)

                def dve_red(v, pan=pan, norm_dst=norm_dst):
                    for i in range(8):
                        idx = pan * 8 + i
                        dcq, half = i // 2, i % 2
                        dc = pan * 4 + dcq
                        v.wait_ge(sPE.h, nmm_marks[idx])
                        v.reduce_max(out=norm_dst[half][:, dc:dc + 1], in_=A[:, :],
                                     axis=AX.X).then_inc(sDVE.h, 1)
                        b.sDVE.v += 1
                block.vector(dve_red)
                assert b.sDVE.v == red_marks[-1]
            return evac_marks, red_marks, st_marks, pan_marks

        def cc(ins_t, outs_t, gate):
            def emit(g):
                for hh, vv in gate:
                    g.wait_ge(hh, vv)
                g.collective_compute("AllGather", ALU.bypass,
                                     replica_groups=[list(range(R))],
                                     ins=[ins_t.ap().opt()],
                                     outs=[outs_t.ap().opt()]).then_inc(sCC.h, 1)
                b.sCC.v += 1
            block.gpsimd(emit)
            return b.sCC.v

        kst_marks_box = []

        def emit_ccK0():
            gate = [(b.sKT[i].h, kst_marks_box[0][4 + i]) for i in range(4)]
            b.m_ccK4[0] = cc(b.kvb_K4[0], b.kvg_K4[0], gate)

        b.m_ccK4 = [None] * 4
        b.m_ccV4 = [None] * 4
        k_evac, k_red, k_st, k_pans = proj("k", b.wkT, b.bk_sb, (b.nmA, b.nmB),
                                           None, pan_hooks={2: emit_ccK0},
                                           st_box=kst_marks_box, pre_pans=k_pre,
                                           h_chunk_marks=h1m)
        m_k_done = b.sPE.v
        kt_after_k = [s.v for s in b.sKT]

        def knfold(v):
            v.wait_ge(sDVE.h, k_red[-1])
            v.wait_ge(sDVE.h, b.sDVE.v)
            v.tensor_tensor(out=b.rowe[:, :], in0=b.nmA[:, 0:16],
                            in1=b.nmB[:, 0:16], op=ALU.max).then_inc(sDVE.h, 1)
            b.sDVE.v += 1
        block.vector(knfold)
        m_knrow = b.sDVE.v

        def cc2_emit(g):
            g.wait_ge(sDVE.h, m_knrow)
            g.dma_start(out=b.kn_b.ap(), in_=b.rowe[:, :]).then_inc(b.sKS.h, 16)
            b.sKS.v += 16
            g.wait_ge(b.sKS.h, b.sKS.v)
            g.collective_compute("AllReduce", ALU.max,
                                 replica_groups=[list(range(R))],
                                 ins=[b.kn_b.ap().opt()],
                                 outs=[b.kn_g.ap().opt()]).then_inc(sCC.h, 1)
            b.sCC.v += 1

        def ldv_emit(blk, gate_val):
            return ld_w_panel(b.wvT, blk, (sPE, gate_val))

        m_vpan_pre = ldv_emit(0, k_pans[2])

        block.gpsimd(cc2_emit)
        b.m_cc2 = b.sCC.v

        v_marks, vst_marks = [], []
        vpan_marks = {}
        for blk in range(4):
            par = blk % 2
            swp = b.sWP[par]
            if blk == 2:
                gate = [(b.sKT[i].h, vst_marks[4 + i]) for i in range(4)]
                b.m_ccV4[0] = cc(b.kvb_V4[0], b.kvg_V4[0], gate)

            if blk == 0:
                m_vpan = m_vpan_pre
            else:
                m_vpan = ldv_emit(blk, m_k_done if blk == 1 else vpan_marks[blk - 2])

            for tt in range(8):
                idx = blk * 8 + tt
                ps = Ep if idx % 2 == 0 else Fp

                def vmm(t, par=par, tt=tt, idx=idx, ps=ps, m_vpan=m_vpan, swp=swp):
                    t.wait_ge(swp.h, m_vpan)
                    if idx <= 1:
                        t.wait_ge(sDVE.h, m_bvb)
                    if idx >= 2:
                        t.wait_ge(sDVE.h, v_marks[idx - 2])
                    for ic in range(NCH):
                        ins = t.matmul(out=ps[:, :],
                                       lhsT=h_all[:, ic * TOK + tt * 128: ic * TOK + (tt + 1) * 128],
                                       rhs=wp[:, par * 8192 + ic * 512: par * 8192 + (ic + 1) * 512],
                                       start=(ic == 0), stop=(ic == NCH - 1))
                    ins.then_inc(sPE.h, 1)
                    b.sPE.v += 1
                block.tensor(vmm)
                m_vmm = b.sPE.v
                if tt == 7:
                    vpan_marks[blk] = m_vmm

                def vev(v, blk=blk, idx=idx, ps=ps, m_vmm=m_vmm):
                    v.wait_ge(sPE.h, m_vmm)
                    if idx >= 4:
                        v.wait_ge(b.sKT[idx % 4].h, vst_marks[idx - 4])
                    else:
                        v.wait_ge(b.sKT[idx % 4].h, kt_after_k[idx % 4])
                    v.scalar_tensor_tensor(out=kst[:, (idx % 4) * HALF:(idx % 4 + 1) * HALF],
                                           in0=ps[:, :], scalar=1.0,
                                           in1=bvb[:, blk * 512:(blk + 1) * 512],
                                           op0=ALU.mult, op1=ALU.add).then_inc(sDVE.h, 1)
                    b.sDVE.v += 1
                block.vector(vev)
                v_marks.append(b.sDVE.v)
                m_vev = b.sDVE.v

                skt = b.sKT[idx % 4]

                def vst(sy, blk=blk, tt=tt, idx=idx, m_vev=m_vev, skt=skt):
                    sy.wait_ge(sDVE.h, m_vev)
                    dst = b.kvb_V4[blk].ap()[tt * 128:(tt + 1) * 128, :]
                    sy.dma_start(out=dst,
                                 in_=kst[:, (idx % 4) * HALF:(idx % 4 + 1) * HALF]).then_inc(skt.h, 16)
                    skt.v += 16
                block.sync(vst)
                vst_marks.append(skt.v)
        m_v_done = b.sPE.v
        b.ccK_late_gate = [(b.sKT[i].h, kt_after_k[i]) for i in range(4)]
        b.ccV_late_gates = [
            [(b.sKT[i].h, vst_marks[blk * 8 + 4 + i]) for i in range(4)]
            for blk in range(1, 4)]

        if "kvb" in b.dbg_d:
            kvb_waits = [(s, s.v) for s in b.sKT]

            def dump_kvb(g):
                for ss, vv in kvb_waits:
                    g.wait_ge(ss.h, vv)
                for i, src_t in enumerate(b.kvb_K4):
                    g.dma_start(out=b.dbg_d["kvb"][i * 512:(i + 1) * 512, :],
                                in_=src_t.ap()).then_inc(b.sDBG.h, 16)
                    b.sDBG.v += 16
                for i, src_t in enumerate(b.kvb_V4):
                    dst = b.dbg_d["kvb"][2048 + i * 512:2048 + (i + 1) * 512, :] \
                        .rearrange("a (b c) -> (a b) c", c=512)
                    g.dma_start(out=dst, in_=src_t.ap()).then_inc(b.sDBG.h, 16)
                    b.sDBG.v += 16
            block.gpsimd(dump_kvb)

        q_evac, q_red, _, q_pans = proj("q", b.wqT, b.bq_sb, (b.nmA, b.nmB),
                                        [(sPE, vpan_marks[2]), (sPE, m_v_done)])
        b.m_q_done = b.sPE.v
        _dump(b, "q", q_all[:, :], sACT, q_evac[-1])

        def qfold(v):
            v.wait_ge(sDVE.h, q_red[-1])
            v.wait_ge(sDVE.h, b.sDVE.v)
            v.tensor_tensor(out=b.qm2[:, :], in0=b.nmA[:, 0:16],
                            in1=b.nmB[:, 0:16], op=ALU.max).then_inc(sDVE.h, 1)
            b.sDVE.v += 1
        block.vector(qfold)
        m_qm2 = b.sDVE.v

        def ldkn(sy):
            sy.wait_ge(sCC.h, b.m_cc2)
            sy.dma_start(out=b.rowe[:, :], in_=b.kn_g.ap()).then_inc(b.sKN.h, 16)
            b.sKN.v += 16
        block.sync(ldkn)
        m_kng = b.sKN.v

        def crow(v):
            v.wait_ge(b.sKN.h, m_kng)
            v.wait_ge(sDVE.h, m_qm2)
            v.tensor_tensor(out=b.rowe[:, :], in0=b.qm2[:, :], in1=b.rowe[:, :],
                            op=ALU.mult).then_inc(sDVE.h, 1)
            b.sDVE.v += 1
        block.vector(crow)
        m_crow = b.sDVE.v

        def csqrt(s):
            s.wait_ge(sDVE.h, m_crow)
            s.activation(out=b.rowe[:, :], in_=b.rowe[:, :], func=AF.Sqrt,
                         scale=1.0 / 128.0).then_inc(sACT.h, 1)
            b.sACT.v += 1
        block.scalar(csqrt)
        m_csq = b.sACT.v

        def cneg(v):
            v.wait_ge(sACT.h, m_csq)
            v.tensor_scalar(out=b.rowe[:, :], in0=b.rowe[:, :], scalar1=-1.0,
                            scalar2=None, op0=ALU.mult).then_inc(sDVE.h, 1)
            b.sDVE.v += 1
        block.vector(cneg)
        m_cneg = b.sDVE.v

        def cbc(t):
            t.wait_ge(sDVE.h, m_cneg)
            t.wait_ge(sACT.h, q_evac[-1])
            t.matmul(out=C[:, 0:16], lhsT=b.onerf[:, :], rhs=b.rowe[:, :],
                     start=True, stop=True).then_inc(sPE.h, 1)
            b.sPE.v += 1
        block.tensor(cbc)
        m_cbc = b.sPE.v

        def ccp(v):
            v.wait_ge(sPE.h, m_cbc)
            v.tensor_copy(out=b.negc[:, :], in_=C[:, 0:16]).then_inc(sDVE.h, 1)
            b.sDVE.v += 1
        block.vector(ccp)
        b.m_negc = b.sDVE.v
        _dump(b, "negc", b.negc[:, :], sDVE, b.m_negc)


ATT_G4 = ([3, 2] * 12 + [3, 1])
assert sum(ATT_G4) == 64


def _attn(b):
    nc, block = b.nc, b.block
    sPE, sACT, sDVE, sCC = b.sPE, b.sACT, b.sDVE, b.sCC
    q_all = b.q_all

    with ExitStack() as scm:
        e = scm.enter_context
        Kb = e(nc.sbuf_tensor("Kb", [128, 2 * 8192], BF16))
        Vb = e(nc.sbuf_tensor("Vb", [128, 2 * 8192], BF16))
        Pb = e(nc.sbuf_tensor("Pb", [128, 3 * 1536], BF16))
        Rg = e(nc.sbuf_tensor("Rg", [128, 1536], BF16))
        Rf = e(nc.sbuf_tensor("Rf", [128, 512], F32))
        drb = e(nc.sbuf_tensor("drb", [128, 512], F32))
        uu = e(nc.sbuf_tensor("uu", [128, 512], F32))
        spsA = e(nc.psum_tensor("spsA", [128, 1536], F32))
        spsB = e(nc.psum_tensor("spsB", [128, 1024], F32))
        ao = [e(nc.psum_tensor(f"ao{i}", [128, 512], F32)) for i in range(2)]
        ebp = e(nc.psum_tensor("ebp", [128, 512], F32))

        NG = len(ATT_G4)
        goff = []
        t0 = 0
        for sz in ATT_G4:
            goff.append((t0, sz))
            t0 += sz
        NIT = 32

        pe0 = b.sPE.v
        act0 = b.sACT.v
        dve0 = b.sDVE.v
        ACT_IT = NG
        DVE_IT = NG + 6

        sched = []
        for i in range(NIT):
            ops = [("S", 0), ("S", 1)]
            if i >= 1:
                ops.append(("dmm", i - 1))
            for g in range(NG):
                if g + 2 < NG:
                    ops.append(("S", g + 2))
                ops.append(("AO", g))
            if i >= 1:
                ops.insert(6, ("bc2", i - 1))
            sched.append(ops)
        sched.append([("dmm", NIT - 1), ("bc2", NIT - 1)])

        S_mark = [dict() for _ in range(NIT)]
        AO_mark = [dict() for _ in range(NIT)]
        DMM_mark = [None] * NIT
        BC2_mark = [None] * NIT
        blk_end = [None] * (NIT + 1)
        c = pe0
        for bi, ops in enumerate(sched):
            for kind, gg in ops:
                c += 1
                if kind == "S":
                    S_mark[bi][gg] = c
                elif kind == "AO":
                    AO_mark[bi][gg] = c
                elif kind == "dmm":
                    DMM_mark[gg] = c
                else:
                    BC2_mark[gg] = c
            blk_end[bi] = c
        expm = [[act0 + i * ACT_IT + g + 1 for g in range(NG)] for i in range(NIT)]
        raccm = [[dve0 + i * DVE_IT + g + 1 for g in range(NG)] for i in range(NIT)]
        foldm = [dve0 + i * DVE_IT + NG + 2 for i in range(NIT)]
        recm = [dve0 + i * DVE_IT + NG + 3 for i in range(NIT)]
        uum = [dve0 + i * DVE_IT + NG + 5 for i in range(NIT)]
        xm = [dve0 + i * DVE_IT + NG + 6 for i in range(NIT)]

        kv_marks = [None] * 16
        kl_marks = [None] * 16
        b.m_x1 = xm[:]

        def late_gathers(g):
            g.wait_ge(sPE.h, blk_end[0])
            for i in range(1, 4):
                for hh, vv in (b.ccK_late_gate if i == 1 else []):
                    g.wait_ge(hh, vv)
                g.collective_compute("AllGather", ALU.bypass,
                                     replica_groups=[list(range(R))],
                                     ins=[b.kvb_K4[i].ap().opt()],
                                     outs=[b.kvg_K4[i].ap().opt()]).then_inc(sCC.h, 1)
                b.sCC.v += 1
                b.m_ccK4[i] = b.sCC.v
                for hh, vv in b.ccV_late_gates[i - 1]:
                    g.wait_ge(hh, vv)
                g.collective_compute("AllGather", ALU.bypass,
                                     replica_groups=[list(range(R))],
                                     ins=[b.kvb_V4[i].ap().opt()],
                                     outs=[b.kvg_V4[i].ap().opt()]).then_inc(sCC.h, 1)
                b.sCC.v += 1
                b.m_ccV4[i] = b.sCC.v
        block.gpsimd(late_gathers)

        if "ffn" in b.phases:
            def stage_ffn(g):
                g.wait_ge(sPE.h, blk_end[7])
                for w_src, w_dst in [(b.gate_sl, b.g8b), (b.up_sl, b.u8b),
                                     (b.down_sl, b.d8b)]:
                    g.dma_start(out=w_dst.ap(), in_=w_src).then_inc(b.sF8.h, 16)
                    b.sF8.v += 16
                g.wait_ge(b.sF8.h, b.sF8.v)
                g.wait_ge(sPE.h, blk_end[11])
                for w8b, w8g in [(b.g8b, b.g8), (b.u8b, b.u8), (b.d8b, b.d8)]:
                    g.collective_compute("AllGather", ALU.bypass,
                                         replica_groups=[list(range(R))],
                                         ins=[w8b.ap().opt()],
                                         outs=[w8g.ap().opt()]).then_inc(sCC.h, 1)
                    b.sCC.v += 1
            block.gpsimd(stage_ffn)
            b.m_ccF8 = b.sCC.v

        for h in range(16):
            hb = h % 2
            skv = b.sKV[hb]
            skl = b.sKL[hb]

            def kload(sy, h=h, hb=hb, skl=skl):
                if h % 4 == 0:
                    sy.wait_ge(sCC.h, b.m_ccK4[h // 4])
                if h >= 2:
                    sy.wait_ge(sPE.h, blk_end[2 * h - 3])
                kg = b.kvg_K4[h // 4]
                for r in range(R):
                    sy.dma_start(
                        out=Kb[:, hb * 8192 + r * 1024:hb * 8192 + (r + 1) * 1024],
                        in_=kg.ap()[r * 512 + (h % 4) * 128:
                                    r * 512 + (h % 4 + 1) * 128, :]
                    ).then_inc(skl.h, 16)
                    skl.v += 16
            block.sync(kload)
            kl_marks[h] = skl.v

            def vload(sy, h=h, hb=hb, skv=skv):
                if h % 4 == 0:
                    sy.wait_ge(sCC.h, b.m_ccV4[h // 4])
                if h >= 2:
                    sy.wait_ge(sPE.h, blk_end[2 * h - 3])
                vg = b.kvg_V4[h // 4]
                for r in range(R):
                    base = (r * 1024) * 512 + (h % 4) * 128
                    src = bass.AP(vg, base, [[512, 128], [128 * 512, 8], [1, 128]])
                    dst = Vb[:, hb * 8192 + r * 1024:hb * 8192 + (r + 1) * 1024] \
                        .rearrange("p (tt d) -> p tt d", d=128)
                    sy.dma_start(out=dst, in_=src).then_inc(skv.h, 16)
                    skv.v += 16
            block.sync(vload)
            kv_marks[h] = skv.v

            for half in range(2):
                it = 2 * h + half
                qs = q_all[:, h * TOK + half * HALF: h * TOK + half * HALF + HALF]
                aop = ao[it % 2]

                def pe_sec(t, h=h, hb=hb, half=half, it=it, qs=qs, aop=aop):
                    for kind, gg in sched[it]:
                        if kind == "S":
                            g0, sz = goff[gg]
                            sps = spsA if gg % 2 == 0 else spsB
                            if gg == 0:
                                t.wait_ge(b.sKL[hb].h, kl_marks[h])
                                if it == 0:
                                    t.wait_ge(sDVE.h, b.m_negc)
                            if gg >= 2:
                                t.wait_ge(sACT.h, expm[it][gg - 2])
                            for ti in range(sz):
                                kt = g0 + ti
                                ins = t.matmul(out=sps[:, ti * 512:(ti + 1) * 512],
                                               lhsT=Kb[:, hb * 8192 + kt * 128:hb * 8192 + (kt + 1) * 128],
                                               rhs=qs, start=True, stop=True)
                            ins.then_inc(sPE.h, 1)
                            b.sPE.v += 1
                        elif kind == "AO":
                            g0, sz = goff[gg]
                            if gg + 2 >= NG:
                                t.wait_ge(sACT.h, expm[it][gg])
                            if gg == 0:
                                t.wait_ge(b.sKV[hb].h, kv_marks[h])
                                if it >= 2:
                                    t.wait_ge(sDVE.h, uum[it - 2])
                            for ti in range(sz):
                                kt = g0 + ti
                                ins = t.matmul(out=aop[:, :],
                                               lhsT=Vb[:, hb * 8192 + kt * 128:hb * 8192 + (kt + 1) * 128],
                                               rhs=Pb[:, (gg % 3) * 1536 + ti * 512:(gg % 3) * 1536 + (ti + 1) * 512],
                                               start=(kt == 0), stop=(kt == 63))
                            ins.then_inc(sPE.h, 1)
                            b.sPE.v += 1
                        elif kind == "dmm":
                            t.wait_ge(sDVE.h, foldm[gg])
                            t.matmul(out=ebp[0:1, :], lhsT=b.onecf[:, :], rhs=Rf[:, :],
                                     start=True, stop=True).then_inc(sPE.h, 1)
                            b.sPE.v += 1
                        else:
                            t.wait_ge(sDVE.h, recm[gg])
                            t.matmul(out=ebp[:, :], lhsT=b.onerf[:, :], rhs=b.rowa[:, :],
                                     start=True, stop=True).then_inc(sPE.h, 1)
                            b.sPE.v += 1
                block.tensor(pe_sec)
                assert b.sPE.v == blk_end[it]

                def act_sec(s, h=h, it=it):
                    for gi in range(NG):
                        g0, sz = goff[gi]
                        sps = spsA if gi % 2 == 0 else spsB
                        s.wait_ge(sPE.h, S_mark[it][gi])
                        if gi >= 3:
                            s.wait_ge(sPE.h, AO_mark[it][gi - 3])
                            s.wait_ge(sDVE.h, raccm[it][gi - 3])
                        elif it > 0:
                            gstar = NG - 1 - ((NG - 1 - gi) % 3)
                            s.wait_ge(sPE.h, AO_mark[it - 1][gstar])
                            s.wait_ge(sDVE.h, raccm[it - 1][gstar])
                        s.activation(out=Pb[:, (gi % 3) * 1536:(gi % 3) * 1536 + sz * 512],
                                     in_=sps[:, 0:sz * 512], func=AF.Exp,
                                     bias=b.negc[:, h:h + 1],
                                     scale=SCALE).then_inc(sACT.h, 1)
                        b.sACT.v += 1
                block.scalar(act_sec)
                assert b.sACT.v == expm[it][NG - 1]

                def dve_sec(v, h=h, half=half, it=it, aop=aop):
                    for gi in range(NG):
                        g0, sz = goff[gi]
                        v.wait_ge(sACT.h, expm[it][gi])
                        seg = Pb[:, (gi % 3) * 1536:(gi % 3) * 1536 + sz * 512]
                        if gi == 0:
                            ins = v.tensor_copy(out=Rg[:, 0:sz * 512], in_=seg)
                        else:
                            v.wait_ge(sDVE.h, raccm[it][gi - 1])
                            ins = v.tensor_tensor(out=Rg[:, 0:sz * 512],
                                                  in0=Rg[:, 0:sz * 512],
                                                  in1=seg, op=ALU.add)
                        ins.then_inc(sDVE.h, 1)
                        b.sDVE.v += 1
                    v.wait_ge(sDVE.h, raccm[it][NG - 1])
                    v.tensor_tensor(out=Rf[:, :], in0=Rg[:, 0:512],
                                    in1=Rg[:, 512:1024], op=ALU.add).then_inc(sDVE.h, 1)
                    b.sDVE.v += 1
                    v.wait_ge(sDVE.h, b.sDVE.v)
                    v.tensor_tensor(out=Rf[:, :], in0=Rf[:, :],
                                    in1=Rg[:, 1024:1536], op=ALU.add).then_inc(sDVE.h, 1)
                    b.sDVE.v += 1
                    v.wait_ge(sPE.h, DMM_mark[it])
                    v.reciprocal(out=b.rowa[:, :], in_=ebp[0:1, :]).then_inc(sDVE.h, 1)
                    b.sDVE.v += 1
                    v.wait_ge(sPE.h, BC2_mark[it])
                    v.wait_ge(sPE.h, AO_mark[it][NG - 1])
                    v.tensor_copy(out=drb[:, :], in_=ebp[:, :]).then_inc(sDVE.h, 1)
                    b.sDVE.v += 1
                    v.wait_ge(sDVE.h, b.sDVE.v)
                    v.tensor_tensor(out=uu[:, :], in0=aop[:, :], in1=drb[:, :],
                                    op=ALU.mult).then_inc(sDVE.h, 1)
                    b.sDVE.v += 1
                    v.wait_ge(sDVE.h, b.sDVE.v)
                    xs = b.x_fm[:, h * TOK + half * HALF: h * TOK + half * HALF + HALF]
                    v.scalar_tensor_tensor(out=xs, in0=uu[:, :],
                                           scalar=b.gacol[:, h:h + 1], in1=xs,
                                           op0=ALU.mult, op1=ALU.add).then_inc(sDVE.h, 1)
                    b.sDVE.v += 1
                block.vector(dve_sec)
                assert b.sDVE.v == xm[it], (b.sDVE.v, xm[it])

        def pe_tail(t):
            t.wait_ge(sDVE.h, foldm[NIT - 1])
            t.matmul(out=ebp[0:1, :], lhsT=b.onecf[:, :], rhs=Rf[:, :],
                     start=True, stop=True).then_inc(sPE.h, 1)
            b.sPE.v += 1
            t.wait_ge(sDVE.h, recm[NIT - 1])
            t.matmul(out=ebp[:, :], lhsT=b.onerf[:, :], rhs=b.rowa[:, :],
                     start=True, stop=True).then_inc(sPE.h, 1)
            b.sPE.v += 1
        block.tensor(pe_tail)
        assert b.sPE.v == blk_end[NIT]
    b.m_attn_done = b.sDVE.v


def _ffn(b):
    nc, block = b.nc, b.block
    sPE, sACT, sDVE, sCC = b.sPE, b.sACT, b.sDVE, b.sCC

    with ExitStack() as scm:
        e = scm.enter_context
        h2 = e(nc.sbuf_tensor("h2_all", [128, NCH * TOK], BF16))
        fuse = e(nc.sbuf_tensor("fuse", [128, 48 * HALF], BF16))
        wpan = e(nc.sbuf_tensor("wpanF", [128, 32768], F8))
        g_sb = e(nc.sbuf_tensor("g_sb", [128, 2 * HALF], BF16))
        xbf = e(nc.sbuf_tensor("ln2_xbf", [128, 2 * HALF], BF16))
        xsq = e(nc.sbuf_tensor("ln2_xsq", [128, 2 * HALF], BF16))
        u_sb = e(nc.sbuf_tensor("ln2_u", [128, 2 * HALF], F32))
        A = e(nc.psum_tensor("f_A", [1, 512], F32))
        Bp = e(nc.psum_tensor("f_B", [1, 512], F32))
        C = e(nc.psum_tensor("f_C", [128, 512], F32))
        Dp = e(nc.psum_tensor("f_D", [128, 512], F32))
        Ep = e(nc.psum_tensor("f_E", [128, 512], F32))
        Fp = e(nc.psum_tensor("f_F", [128, 512], F32))
        Gp = e(nc.psum_tensor("f_G", [128, 512], F32))
        Hp = e(nc.psum_tensor("f_H", [128, 512], F32))

        sc = {"A": A, "B": Bp, "C": Gp, "D": Hp, "xbf": xbf, "xsq": xsq, "u": u_sb}
        h2m = _ln(b, sc, h2, s_off=32, c_off=48)
        b.m_h2 = b.sDVE.v
        _dump(b, "h2", h2[:, :], sDVE, b.m_h2)
        m_ln2_apply = sc["apply_done"][1]

        last_down_mark = {0: None, 1: None}
        down_subs = {0: None, 1: None}
        B_ROFF = [0, 16384, 8192, 24576]
        for half in range(2):
            ffnA_pe0 = b.sPE.v
            ffnA_act0 = b.sACT.v
            ffnA_dve0 = b.sDVE.v
            gmm_mark = [ffnA_pe0 + (i // 4) * 8 + (i % 4) * 2 + 1 for i in range(48)]
            umm_mark = [ffnA_pe0 + (i // 4) * 8 + (i % 4) * 2 + 2 for i in range(48)]
            evg_mark = [ffnA_act0 + i + 1 for i in range(48)]
            fus_mark = [ffnA_dve0 + i + 1 for i in range(48)]
            pan_marks = {}
            for sp in range(12):
                par = sp % 2
                sfp = b.sFP[par]

                def lda(sy, sp=sp, par=par, half=half, sfp=sfp):
                    if sp >= 2:
                        sy.wait_ge(sPE.h, pan_marks[sp - 2])
                    elif half == 1:
                        sy.wait_ge(sPE.h, down_subs[0][13 if sp == 0 else 15])
                    if sp < 2 and half == 0:
                        sy.wait_ge(b.sCC.h, b.m_ccF8)
                    for w, w8 in ((0, b.g8), (1, b.u8)):
                        for g4 in range(4):
                            src = bass.AP(w8, (g4 * 512) * FF + sp * 512,
                                          [[FF, 512], [1, 512]]) \
                                .rearrange("(ic p) o -> p ic o", p=128)
                            dst = wpan[:, w * 16384 + par * 8192 + g4 * 2048:
                                       w * 16384 + par * 8192 + (g4 + 1) * 2048] \
                                .rearrange("p (ic o) -> p ic o", o=512)
                            sy.dma_start(out=dst, in_=src).then_inc(sfp.h, 16)
                            sfp.v += 16
                block.sync(lda)
                m_pan = sfp.v

                def pe_sec(t, sp=sp, par=par, m_pan=m_pan, half=half, sfp=sfp):
                    for icq in range(4):
                        idx = sp * 4 + icq
                        gps = C if idx % 2 == 0 else Dp
                        ups = Ep if idx % 2 == 0 else Fp
                        t.wait_ge(sfp.h, m_pan)
                        if idx >= 2:
                            t.wait_ge(sACT.h, evg_mark[idx - 2])
                        for ic in range(NCH):
                            if idx == 0:
                                t.wait_ge(sDVE.h, h2m[half][ic])
                            ins = t.matmul(out=gps[:, :],
                                           lhsT=wpan[:, par * 8192 + ic * 512 + icq * 128:
                                                     par * 8192 + ic * 512 + (icq + 1) * 128],
                                           rhs=h2[:, ic * TOK + half * HALF:
                                                  ic * TOK + half * HALF + HALF],
                                           start=(ic == 0), stop=(ic == NCH - 1))
                        ins.then_inc(sPE.h, 1)
                        b.sPE.v += 1
                        if idx >= 2:
                            t.wait_ge(sDVE.h, fus_mark[idx - 2])
                        for ic in range(NCH):
                            ins = t.matmul(out=ups[:, :],
                                           lhsT=wpan[:, 16384 + par * 8192 + ic * 512 + icq * 128:
                                                     16384 + par * 8192 + ic * 512 + (icq + 1) * 128],
                                           rhs=h2[:, ic * TOK + half * HALF:
                                                  ic * TOK + half * HALF + HALF],
                                           start=(ic == 0), stop=(ic == NCH - 1))
                        ins.then_inc(sPE.h, 1)
                        b.sPE.v += 1
                block.tensor(pe_sec)
                assert b.sPE.v == umm_mark[sp * 4 + 3]
                pan_marks[sp] = b.sPE.v

                def act_sec(s, sp=sp):
                    for icq in range(4):
                        idx = sp * 4 + icq
                        gps = C if idx % 2 == 0 else Dp
                        s.wait_ge(sPE.h, gmm_mark[idx])
                        if idx >= 2:
                            s.wait_ge(sDVE.h, fus_mark[idx - 2])
                        s.activation(out=g_sb[:, (idx % 2) * HALF:(idx % 2 + 1) * HALF],
                                     in_=gps[:, :], func=AF.Silu,
                                     scale=b.gsc_sb[:, idx:idx + 1]).then_inc(sACT.h, 1)
                        b.sACT.v += 1
                block.scalar(act_sec)
                assert b.sACT.v == evg_mark[sp * 4 + 3]

                def dve_sec(v, sp=sp):
                    for icq in range(4):
                        idx = sp * 4 + icq
                        ups = Ep if idx % 2 == 0 else Fp
                        v.wait_ge(sPE.h, umm_mark[idx])
                        v.wait_ge(sACT.h, evg_mark[idx])
                        v.scalar_tensor_tensor(
                            out=fuse[:, idx * HALF:(idx + 1) * HALF],
                            in0=ups[:, :], scalar=b.usc_sb[:, idx:idx + 1],
                            in1=g_sb[:, (idx % 2) * HALF:(idx % 2 + 1) * HALF],
                            op0=ALU.mult, op1=ALU.mult).then_inc(sDVE.h, 1)
                        b.sDVE.v += 1
                block.vector(dve_sec)
                assert b.sDVE.v == fus_mark[sp * 4 + 3]
            m_fuse_all = fus_mark[-1]
            m_phaseA_pe = b.sPE.v

            dn_marks = []
            sub_mm_marks = []
            subi = 0
            for hc2 in range(8):
                c0 = hc2 * 256
                m_subs = []
                for ip in range(2):
                    r4 = subi % 4
                    sfp = b.sFPB[r4]
                    roff = B_ROFF[r4]

                    def ldd(sy, ip=ip, roff=roff, c0=c0, subi=subi, sfp=sfp):
                        if subi >= 4:
                            sy.wait_ge(sPE.h, sub_mm_marks[subi - 4])
                        elif subi < 2:
                            sy.wait_ge(sPE.h, pan_marks[10])
                        else:
                            sy.wait_ge(sPE.h, m_phaseA_pe)
                        for q4 in range(4):
                            src = bass.AP(b.d8, (ip * 3072 + q4 * 768) * D + c0,
                                          [[D, 768], [1, 256]]) \
                                .rearrange("(icl p) u -> p icl u", p=128)
                            dst = wpan[:, roff + 6 * q4 * 256:
                                       roff + (6 * q4 + 6) * 256] \
                                .rearrange("p (icl u) -> p icl u", u=256)
                            sy.dma_start(out=dst, in_=src).then_inc(sfp.h, 16)
                            sfp.v += 16
                    block.sync(ldd)
                    m_subs.append(sfp.v)
                    subi += 1

                pairs = [(C, Dp), (Ep, Fp), (Gp, Hp)]
                pj = hc2 % 3
                for ip in range(2):
                    r4 = (subi - 2 + ip) % 4
                    roff = B_ROFF[r4]
                    m_dpan = m_subs[ip]
                    sfp = b.sFPB[r4]

                    def dmmn(t, hc2=hc2, ip=ip, roff=roff, m_dpan=m_dpan,
                             half=half, sfp=sfp, pj=pj):
                        t.wait_ge(sfp.h, m_dpan)
                        if ip == 0:
                            t.wait_ge(sDVE.h, m_fuse_all)
                            if hc2 >= 3:
                                t.wait_ge(sDVE.h, dn_marks[2 * (hc2 - 3) + 1])
                        for icd in range(24):
                            icg = ip * 24 + icd
                            for hq in range(2):
                                ps = pairs[pj][hq]
                                ins = t.matmul(out=ps[:, :],
                                               lhsT=wpan[:, roff + icd * 256 + hq * 128:
                                                         roff + icd * 256 + (hq + 1) * 128],
                                               rhs=fuse[:, icg * HALF:(icg + 1) * HALF],
                                               start=(icg == 0), stop=(icg == 47))
                        ins.then_inc(sPE.h, 1)
                        b.sPE.v += 1
                    block.tensor(dmmn)
                    sub_mm_marks.append(b.sPE.v)
                m_dn = b.sPE.v

                for hq in range(2):
                    hc = hc2 * 2 + hq
                    ps = pairs[pj][hq]

                    def dupd(v, hc=hc, half=half, ps=ps, m_dn=m_dn):
                        v.wait_ge(sPE.h, m_dn)
                        xs = b.x_fm[:, hc * TOK + half * HALF: hc * TOK + half * HALF + HALF]
                        v.scalar_tensor_tensor(out=xs, in0=ps[:, :],
                                               scalar=b.gbds[:, hc:hc + 1], in1=xs,
                                               op0=ALU.mult, op1=ALU.add).then_inc(sDVE.h, 1)
                        b.sDVE.v += 1
                    block.vector(dupd)
                    dn_marks.append(b.sDVE.v)
                    m_upd = b.sDVE.v

                    def outst(sy, hc=hc, half=half, m_upd=m_upd):
                        sy.wait_ge(sDVE.h, m_upd)
                        sy.dma_start(
                            out=b.out_d[hc * 128:(hc + 1) * 128,
                                        half * HALF:(half + 1) * HALF],
                            in_=b.x_fm[:, hc * TOK + half * HALF:
                                       hc * TOK + half * HALF + HALF]).then_inc(b.sST.h, 16)
                        b.sST.v += 16
                    block.sync(outst)
            last_down_mark[half] = b.sPE.v
            down_subs[half] = sub_mm_marks


def shard_inputs(ins):
    x = np.asarray(ins["L0_attn_x"])[0]
    xT = np.ascontiguousarray(x.T)
    wqT = np.ascontiguousarray(np.asarray(ins["L0_attn_Wq"]).T)
    wkT = np.ascontiguousarray(np.asarray(ins["L0_attn_Wk"]).T)
    wvT = np.ascontiguousarray(np.asarray(ins["L0_attn_Wv"]).T)
    bqc = np.ascontiguousarray(np.asarray(ins["L0_attn_bq"]).reshape(16, 128).T)
    bkc = np.ascontiguousarray(np.asarray(ins["L0_attn_bk"]).reshape(16, 128).T)
    bvr = np.ascontiguousarray(np.asarray(ins["L0_attn_bv"]).reshape(1, 2048))
    gscc = np.ascontiguousarray(np.asarray(ins["L0_ffn_gate_scale"]).reshape(48, 128).T)
    uscc = np.ascontiguousarray(np.asarray(ins["L0_ffn_up_scale"]).reshape(48, 128).T)
    dscc = np.ascontiguousarray(np.asarray(ins["L0_ffn_down_scale"]).reshape(16, 128).T)
    cvec = np.ascontiguousarray(np.asarray(ins["L0_c"]).reshape(1, 2048))
    wmod = np.asarray(ins["L0_wmod"])
    gate_w = np.asarray(ins["L0_ffn_gate_w"])
    up_w = np.asarray(ins["L0_ffn_up_w"])
    down_w = np.asarray(ins["L0_ffn_down_w"])
    id_f32 = np.eye(128, dtype=np.float32)
    maps = []
    for c in range(R):
        maps.append({
            "xT": np.ascontiguousarray(xT[:, c * TOK:(c + 1) * TOK]),
            "wqT": wqT, "wkT": wkT, "wvT": wvT,
            "bqc": bqc, "bkc": bkc, "bvr": bvr,
            "gscc": gscc, "uscc": uscc, "dscc": dscc,
            "cvec": cvec,
            "wmod_sl": np.ascontiguousarray(wmod[:, c * 1536:(c + 1) * 1536]),
            "gate_sl": np.ascontiguousarray(gate_w[c * 256:(c + 1) * 256, :]),
            "up_sl": np.ascontiguousarray(up_w[c * 256:(c + 1) * 256, :]),
            "down_sl": np.ascontiguousarray(down_w[c * 768:(c + 1) * 768, :]),
            "id_f32": id_f32,
            "ones_colb": np.ones((128, 1), dtype=ml_dtypes.bfloat16),
            "ones_colf": np.ones((128, 1), dtype=np.float32),
            "ones_rowf": np.ones((1, 128), dtype=np.float32),
        })
    return maps


def kernel(**inputs):
    maps = shard_inputs(inputs)
    nc, _ = build()
    res = run_bass_kernel_spmd(nc, maps, core_ids=list(range(R)))
    parts = [np.asarray(r["out"]).T for r in res.results]
    return np.concatenate(parts, axis=0).reshape(1, L, D).astype(np.float32)
